# version 03
# baseline (speedup 1.0000x reference)
import contextlib
import numpy as np
import ml_dtypes
import concourse.bass as bass
import concourse.mybir as mybir
import concourse.bass_utils as _bu
from concourse.bass_utils import run_bass_kernel_spmd
from concourse.alu_op_type import AluOpType as OP

F32 = mybir.dt.float32
BF16 = mybir.dt.bfloat16
U32 = mybir.dt.uint32
AF = mybir.ActivationFunctionType

_orig_run_command = _bu.run_command


def _patched_run_command(cmd, *a, **k):
    if isinstance(cmd, list):
        cmd = ["--enable-birsim=false" if c == "--enable-birsim=true" else c for c in cmd]
    return _orig_run_command(cmd, *a, **k)


_bu.run_command = _patched_run_command

T = 2048
NT = 16
D = 4096
NK = 32
H = 32
KVH = 8
HD = 128
F = 11008
NF = 86
GU = 2 * F
EPS = 1e-5

DO_ATTN = True
DO_FFN = True
DBG_C = 21


class Prog:

    def __init__(self, sems):
        self.semnames = list(sems)
        self.sems = sems
        self.eng = None
        self.engobj = None
        self.cnt = {}

    def run(self, engname, engobj, build):
        self.eng = engname
        self.engobj = engobj
        self.cnt = {s: 0 for s in self.semnames}
        build(self)

    def do(self, engname, fn, inc=None, inc_n=1):
        if inc is not None:
            self.cnt[inc] += inc_n
        if self.eng == engname:
            inst = fn(self.engobj)
            if inc is not None and inst is not None:
                inst.then_inc(self.sems[inc], inc_n)

    def dma(self, engname, fn, q):
        self.do(engname, fn, inc=q, inc_n=16)

    def wait(self, engname, semname, val=None):
        v = self.cnt[semname] if val is None else val
        if self.eng == engname:
            self.engobj.wait_ge(self.sems[semname], v)
        return v

    def val(self, semname):
        return self.cnt[semname]

    OWN = {"ve": "vS", "sc": "aS", "te": "tS", "po": "poS"}

    def dof(self, engname, fn):
        own = Prog.OWN[engname]
        self.do(engname, fn, inc=own)
        self.wait(engname, own)

    def barrier(self):
        for e in ["sy", "te", "ve", "sc", "po"]:
            for s in self.semnames:
                self.wait(e, s)



def _host_constants():
    c = {}
    c["c_ident16"] = np.eye(128, dtype=ml_dtypes.bfloat16)
    c["c_identf"] = np.eye(128, dtype=np.float32)
    inv_freq = (1.0 / 10000.0 ** (np.arange(0, HD, 2, dtype=np.float64) / HD))
    ang = np.arange(T, dtype=np.float64)[:, None] * inv_freq[None, :]
    cos = np.cos(ang)
    sin = np.sin(ang)
    c["c_cosT"] = np.concatenate([cos, cos], axis=1).T.astype(ml_dtypes.bfloat16)
    c["c_sinT"] = np.concatenate([-sin, sin], axis=1).T.astype(ml_dtypes.bfloat16)
    jj = np.arange(128)[:, None]
    ii = np.arange(128)[None, :]
    c["c_maskT"] = np.where(ii >= jj, 0.0, -1e30).astype(np.float32)
    c["c_shifts"] = (4 * (np.arange(128, dtype=np.uint32) % 8)).reshape(128, 1)
    c["c_shift8"] = np.tile((4 * np.arange(8, dtype=np.uint32))[None, :], (128, 1))
    c["c_ones1"] = np.ones((1, 128), dtype=ml_dtypes.bfloat16)
    c["c_ones1f"] = np.ones((1, 128), dtype=np.float32)
    c["c_ones128"] = np.ones((128, 1), dtype=ml_dtypes.bfloat16)
    c["c_ones128f"] = np.ones((128, 1), dtype=np.float32)
    sel = np.zeros((128, NK * NK), dtype=ml_dtypes.bfloat16)
    for m in range(NK):
        sel[:, NK * m + m] = 1.0
    c["c_selbank"] = sel
    eb = np.zeros((16, 16 * 128), dtype=np.float32)
    for n in range(16):
        eb[n, 128 * n:128 * (n + 1)] = 1.0
    c["c_eb16"] = eb
    return c



def build_program(debug=()):
    nc = bass.Bass("TRN2", target_bir_lowering=False)

    def din(name, shape, dt):
        return nc.dram_tensor(name, shape, dt, kind="ExternalInput").ap()

    E = {}
    E["x"] = din("x", [T, D], F32)
    E["wq"] = din("L0_attn_Wq", [D, D], F32)
    E["wk"] = din("L0_attn_Wk", [KVH * HD, D], F32)
    E["wv"] = din("L0_attn_Wv", [KVH * HD, D], F32)
    E["wo"] = din("L0_attn_Wo", [D, D], F32)
    E["wg"] = din("L0_attn_Wg", [D, D], F32)
    E["win"] = din("L0_attn_Win", [H, D], F32)
    E["dtb"] = din("dtb", [H, 1], F32)
    E["alog"] = din("alog", [H, 1], F32)
    E["gnw"] = din("gnw", [128, 1], F32)
    E["qw_gu"] = din("L0_ffn_qweight_gu", [512, GU], U32)
    E["qz_gu"] = din("L0_ffn_qzeros_gu", [32, GU // 8], U32)
    E["sc_gu"] = din("L0_ffn_scales_gu", [32, GU], F32)
    E["qw_d"] = din("L0_ffn_qweight_down", [F // 8, D], U32)
    E["qz_d"] = din("L0_ffn_qzeros_down", [NF, D // 8], U32)
    E["sc_d"] = din("L0_ffn_scales_down", [NF, D], F32)
    E["ident16"] = din("c_ident16", [128, 128], BF16)
    E["identf"] = din("c_identf", [128, 128], F32)
    E["cosT"] = din("c_cosT", [128, T], BF16)
    E["sinT"] = din("c_sinT", [128, T], BF16)
    E["maskT"] = din("c_maskT", [128, 128], F32)
    E["shifts"] = din("c_shifts", [128, 1], U32)
    E["shift8"] = din("c_shift8", [128, 8], U32)
    E["ones1"] = din("c_ones1", [1, 128], BF16)
    E["ones1f"] = din("c_ones1f", [1, 128], F32)
    E["ones128"] = din("c_ones128", [128, 1], BF16)
    E["ones128f"] = din("c_ones128f", [128, 1], F32)
    E["selbank"] = din("c_selbank", [128, NK * NK], BF16)
    E["eb16"] = din("c_eb16", [16, 16 * 128], F32)

    out = nc.dram_tensor("out", [T, D], F32, kind="ExternalOutput").ap()
    dbg = {}
    for name, shape, dt in debug:
        dbg[name] = nc.dram_tensor(name, shape, dt, kind="ExternalOutput").ap()

    xn16 = nc.dram_tensor("xn16_h", [T, D], BF16, kind="Internal").ap()
    w16 = {}
    for nm, rows in [("wq", D), ("wk", KVH * HD), ("wv", KVH * HD),
                     ("wg", D), ("wo", D), ("win", H)]:
        w16[nm] = nc.dram_tensor(nm + "16_h", [rows, D], BF16, kind="Internal").ap()
    wqT_h = nc.dram_tensor("wqT_h", [H * D, 128], BF16, kind="Internal").ap()
    wgT_h = nc.dram_tensor("wgT_h", [H * D, 128], BF16, kind="Internal").ap()
    wkT_h = nc.dram_tensor("wkT_h", [KVH * D, 128], BF16, kind="Internal").ap()
    wvT_h = nc.dram_tensor("wvT_h", [KVH * D, 128], BF16, kind="Internal").ap()
    woT_h = nc.dram_tensor("woT_h", [D, D], BF16, kind="Internal").ap()
    wd16_h = nc.dram_tensor("wd16_h", [F, D], BF16, kind="Internal").ap()
    zd16_h = nc.dram_tensor("zd16_h", [NF, D], BF16, kind="Internal").ap()
    hT_h = nc.dram_tensor("hT_h", [F, T], BF16, kind="Internal").ap()
    hbar_h = nc.dram_tensor("hbar_h", [NF, T], BF16, kind="Internal").ap()
    ytn_h = nc.dram_tensor("ytn_h", [D, T], BF16, kind="Internal").ap()

    stack = contextlib.ExitStack()

    def salloc(name, shape, dt, side="left"):
        g = nc.sbuf_tensor("s_" + name, shape, dt, side=side)
        t = g.__enter__()
        return t, g

    def palloc(name, shape, dt):
        g = nc.psum_tensor("p_" + name, shape, dt)
        t = g.__enter__()
        return t, g

    def free(*guards):
        for g in guards:
            g.__exit__(None, None, None)

    GT = {}
    for nm, shape, dt in [
        ("ident16", [128, 128], BF16),
        ("shifts", [128, 1], U32), ("shift8", [128, 8], U32),
        ("ones1", [1, 128], BF16), ("ones1f", [1, 128], F32),
        ("ones128", [128, 1], BF16), ("ones128f", [128, 1], F32),
        ("gnw", [128, 1], F32), ("dtb", [H, 1], F32), ("alog", [H, 1], F32),
        ("sbarT", [NK, T], BF16),
    ]:
        GT[nm], _ = salloc(nm, shape, dt)
    RT = {}
    rg = {}
    for nm, shape, dt in [
        ("identf", [128, 128], F32),
        ("maskT", [128, 128], F32),
        ("eb16", [16, 16 * 128], F32),
        ("G_tm", [128, NT * H], F32), ("dt_tm", [128, NT * H], F32),
    ]:
        RT[nm], rg[nm] = salloc(nm, shape, dt, side="right")

    xnT, g_xnT = salloc("xnT", [128, NK * T], BF16)

    sem_names = ["tS", "vS", "aS", "poS",
                 "qc", "qck", "qcv", "qcw", "qcq", "qcg", "qco",
                 "qx0", "qx1", "qxs", "qxt",
                 "qzd", "qzdst", "qrep0", "qrep1", "qscd0", "qscd1", "qwst0", "qwst1",
                 "qsel", "qwin", "qwt", "qcs", "qrot", "qyt", "qtr0", "qtr1", "qst5",
                 "qyq", "qw2a", "qw2b", "qxr0", "qxr1", "qo0", "qo1",
                 "qr30", "qr31", "qsc30", "qsc31", "qz3", "qhst", "qhbr",
                 "qhT", "qwd0", "qwd1", "qwd2", "qzs4", "qhbl", "qor0", "qor1",
                 "qo4a", "qo4b", "qdbg"]
    sems = {s: stack.enter_context(nc.semaphore(s)) for s in sem_names}
    p = Prog(sems)

    stg5 = []
    g05 = []
    for i in range(2):
        t, g = salloc(f"stg5_{i}", [128, 512], BF16); g05.append(g); stg5.append(t)
    p1g = []
    xbuf, xnbuf = [], []
    t, g = salloc("xbuf0", [128, D], F32); p1g.append(g); xbuf.append(t); xbuf.append(t)
    t, g = salloc("xnbuf0", [128, D], BF16); p1g.append(g); xnbuf.append(t); xnbuf.append(t)
    sqdump, g = salloc("sqdump", [128, D], F32); p1g.append(g)
    cols = {}
    for nm in ["sum", "sq", "mean", "rstd", "t0"]:
        cols[nm], g = salloc("col_" + nm, [128, 2], F32); p1g.append(g)
    free(*reversed(p1g))

    p15a = []
    qzd_sb, g = salloc("qzd", [NF, D // 16], U32); p15a.append(g)
    zqu_sb, g = salloc("zqu", [NF, D // 2], U32); p15a.append(g)
    zqf_sb, g = salloc("zqf", [NF, D // 2], F32); p15a.append(g)
    scd_sb, g = salloc("scd", [NF, D // 2], F32); p15a.append(g)
    zd16_sb, g = salloc("zd16", [NF, D // 2], BF16); p15a.append(g)
    free(*reversed(p15a))
    p15b = []
    qdbuf = []
    for i in range(2):
        t, g = salloc(f"qdbuf{i}", [128, D // 2], U32); p15b.append(g); qdbuf.append(t)
    wdbuf = []
    for i in range(2):
        t, g = salloc(f"wdbuf{i}", [128, D], BF16); p15b.append(g); wdbuf.append(t)
    scr16d = []
    for i in range(2):
        t, g = salloc(f"scr16d{i}", [1, D // 2], BF16); p15b.append(g); scr16d.append(t)
    free(*reversed(p15b))

    prep = []
    selbank_sb, g = salloc("selbank", [128, NK * NK], BF16); prep.append(g)
    win16_sb, g = salloc("win16", [H, D], BF16); prep.append(g)
    winT16, g = salloc("winT16", [128, H * NK], BF16); prep.append(g)
    dtraw, g = salloc("dtraw", [H, T], F32); prep.append(g)
    spt1, g = salloc("spt1", [H, T], F32); prep.append(g)
    spt2, g = salloc("spt2", [H, T], F32); prep.append(g)
    acol, g = salloc("acol", [H, 1], F32); prep.append(g)
    free(*reversed(prep))

    free(*reversed(g05))
    p2g = []
    def a2(nm, shape, dt):
        t, g = salloc(nm, shape, dt)
        p2g.append(g)
        return t
    WT16 = a2("WT16", [128, D], BF16)
    kT16 = a2("kT16", [128, T], BF16)
    ktm16 = a2("ktm16", [128, T], BF16)
    vtm16 = a2("vtm16", [128, T], BF16)
    qT16 = a2("qT16", [128, T], BF16)
    y16 = a2("y16", [128, T], BF16)
    S_all = a2("S_all", [128, T], F32)
    sdec_all = a2("sdec_all", [128, T], BF16)
    r1 = a2("r1", [128, 512], F32)
    r2 = a2("r2", [128, 512], F32)
    cs_sl = a2("cs_sl", [128, 512], BF16)
    sn_sl = a2("sn_sl", [128, 512], BF16)
    gatechunk = a2("gatechunk", [128, 512], BF16)
    ynst = a2("ynst", [128, 512], BF16)
    grow_sb = a2("grow", [16, 128], F32)
    glrow = a2("glrow", [1, 16], F32)
    fmat = a2("fmat", [128, 16], F32)
    cdall = a2("cdall", [128, 16], F32)
    xc16 = a2("xc16", [128, 128], BF16)
    kdec16 = a2("kdec16", [128, 128], BF16)
    s16c = a2("s16c", [128, 128], BF16)
    expg16 = a2("expg16", [128, 128], BF16)
    dec_scr = a2("dec_scr", [128, 128], F32)
    sq_scr = a2("sq_scr", [128, 128], F32)
    ssum = a2("ssum", [1, T], F32)
    rinv = ssum
    free(*reversed(p2g))

    p2b = []
    ytq = []
    for i in range(NK):
        t, g = salloc(f"ytq{i}", [128, 512], BF16); p2b.append(g); ytq.append(t)
    WT2 = []
    for i in range(2):
        t, g = salloc(f"WT2_{i}", [128, 8 * 512], BF16); p2b.append(g); WT2.append(t)
    xrd, outw = [], []
    for i in range(2):
        t, g = salloc(f"xrd{i}", [128, 512], F32); p2b.append(g); xrd.append(t)
        t, g = salloc(f"outw{i}", [128, 512], F32); p2b.append(g); outw.append(t)
    free(*reversed(p2b))

    free(*[rg[nm] for nm in ["dt_tm", "G_tm", "eb16", "maskT", "identf"]])
    p3g = []
    def a3(nm, shape, dt):
        t, g = salloc(nm, shape, dt)
        p3g.append(g)
        return t
    W1g = a3("W1g", [128, 33 * 256], BF16)
    W1u = a3("W1u", [128, 33 * 256], BF16)
    qbuf = [a3("qbuf0", [128, 1024], U32), a3("qbuf1", [128, 1024], U32)]
    scrow3 = [a3("scrow0", [1, 1024], BF16), a3("scrow1", [1, 1024], BF16)]
    zpk = a3("zpk", [32, 32], U32)
    zun = a3("zun", [32, 256], U32)
    zf = a3("zf", [32, 256], F32)
    zs = a3("zs", [32, 256], F32)
    zb16 = {"g": a3("zb16g", [32, 256], BF16), "u": a3("zb16u", [32, 256], BF16)}
    gstash = [a3(f"gstash{i}", [128, T], BF16) for i in range(2)]
    h16sb = a3("h16sb", [128, T], BF16)
    hrow = a3("hrow", [1, T], BF16)
    free(*reversed(p3g))

    free(g_xnT)
    p4g = []
    def a4(nm, shape, dt):
        t, g = salloc(nm, shape, dt)
        p4g.append(g)
        return t
    hTh = a4("hTh", [128, NF * 1024], BF16)
    wdt = [a4(f"wdt{i}", [128, 512], BF16) for i in range(3)]
    zdsl = a4("zdsl", [NF, 512], BF16)
    hbarL = a4("hbarL", [NF, T], BF16)
    ord_ = [a4(f"ord{i}", [128, 512], F32) for i in range(2)]
    owr_ = [a4(f"owr{i}", [128, 512], F32) for i in range(2)]
    free(*reversed(p4g))

    psA, g_psA = palloc("psA", [128, 2048], F32)
    psB, g_psB = palloc("psB", [128, 2048], F32)
    free(g_psB, g_psA)
    ps_proj, q1 = palloc("ps_proj", [128, 1024], F32)
    ps_gb, q2 = palloc("ps_gb", [128, 128], F32)
    ps_sT, q3 = palloc("ps_sT", [128, 128], F32)
    ps_cs, q4 = palloc("ps_cs", [128, 128], F32)
    ps_y, q5 = palloc("ps_y", [128, 128], F32)
    ps_t16, q6 = palloc("ps_t16", [128, 128], BF16)
    ps_tf, q7 = palloc("ps_tf", [128, 128], F32)
    free(q7, q6, q5, q4, q3, q2, q1)
    psD, g_psD = palloc("psD", [128, 4096], F32)
    free(g_psD)

    CNT = {}

    def phase05(p):
        castq5 = {"wk": "qck", "wv": "qcv", "win": "qcw", "wq": "qcq", "wg": "qcg", "wo": "qco"}
        st_marks = {}
        gi = 0
        for wname, dstT, nob in [("wk", wkT_h, 2), ("wv", wvT_h, 2),
                                 ("wq", wqT_h, 8), ("wg", wgT_h, 8), ("wo", woT_h, 8)]:
            p.wait("sy", castq5[wname], CNT["cast_" + wname])
            for ob in range(nob):
                for kt in range(NK):
                    par = gi % 2
                    qtr = "qtr0" if par == 0 else "qtr1"
                    if gi >= 2:
                        p.wait("sy", "qst5", st_marks[gi - 2])
                    p.dma("sy", lambda e, wname=wname, ob=ob, kt=kt, par=par: e.dma_start(
                        out=stg5[par][:, :],
                        in_=w16[wname][512 * ob:512 * (ob + 1), 128 * kt:128 * (kt + 1)],
                        transpose=True), qtr)
                    trv = p.val(qtr)
                    p.wait("sc", qtr, trv)
                    if wname == "wo":
                        p.dma("sc", lambda e, ob=ob, kt=kt, par=par: e.dma_start(
                            out=woT_h[128 * kt:128 * (kt + 1), 512 * ob:512 * (ob + 1)],
                            in_=stg5[par][:, :]), "qst5")
                    else:
                        for j in range(4):
                            hb = 4 * ob + j
                            p.dma("sc", lambda e, dstT=dstT, hb=hb, kt=kt, j=j, par=par: e.dma_start(
                                out=dstT[D * hb + 128 * kt:D * hb + 128 * (kt + 1), :],
                                in_=stg5[par][:, 128 * j:128 * (j + 1)]), "qst5")
                    st_marks[gi] = p.val("qst5")
                    gi += 1
            CNT["pt_" + wname] = p.val("qst5")

    def phase1(p):
        ln_done, st_done = [], []
        for i in range(NT):
            b = i % 2
            if i >= 1:
                p.wait("sy", "vS", ln_done[i - 1])
            qx = "qx0" if b == 0 else "qx1"
            p.dma("sy", lambda e, i=i, b=b: e.dma_start(out=xbuf[b][:, :],
                                                        in_=E["x"][128 * i:128 * (i + 1), :]), qx)
            ldv = p.val(qx)
            p.wait("ve", qx, ldv)
            p.do("ve", lambda e, b=b: e.reduce_sum(out=cols["sum"][:, b:b + 1],
                                                   in_=xbuf[b][:, :], axis=mybir.AxisListType.X))
            p.wait("sc", qx, ldv)
            p.do("sc", lambda e, b=b: e.activation(out=sqdump[:, :], in_=xbuf[b][:, :],
                                                   func=AF.Square), inc="aS")
            p.wait("ve", "aS")
            p.do("ve", lambda e, b=b: e.reduce_sum(out=cols["sq"][:, b:b + 1],
                                                   in_=sqdump[:, :], axis=mybir.AxisListType.X))
            p.do("ve", lambda e, b=b: e.tensor_scalar_mul(out=cols["mean"][:, b:b + 1],
                                                          in0=cols["sum"][:, b:b + 1], scalar1=1.0 / D))
            p.do("ve", lambda e, b=b: e.tensor_scalar_mul(out=cols["sq"][:, b:b + 1],
                                                          in0=cols["sq"][:, b:b + 1], scalar1=1.0 / D))
            p.do("ve", lambda e, b=b: e.tensor_mul(out=cols["t0"][:, b:b + 1],
                                                   in0=cols["mean"][:, b:b + 1],
                                                   in1=cols["mean"][:, b:b + 1]))
            p.do("ve", lambda e, b=b: e.tensor_sub(out=cols["sq"][:, b:b + 1],
                                                   in0=cols["sq"][:, b:b + 1],
                                                   in1=cols["t0"][:, b:b + 1]))
            p.do("ve", lambda e, b=b: e.tensor_scalar_add(out=cols["sq"][:, b:b + 1],
                                                          in0=cols["sq"][:, b:b + 1], scalar1=EPS), inc="vS")
            p.wait("sc", "vS")
            p.do("sc", lambda e, b=b: e.activation(out=cols["rstd"][:, b:b + 1],
                                                   in_=cols["sq"][:, b:b + 1], func=AF.Ln), inc="aS")
            p.wait("ve", "aS")
            p.do("ve", lambda e, b=b: e.tensor_scalar_mul(out=cols["rstd"][:, b:b + 1],
                                                          in0=cols["rstd"][:, b:b + 1], scalar1=-0.5), inc="vS")
            p.wait("sc", "vS")
            p.do("sc", lambda e, b=b: e.activation(out=cols["rstd"][:, b:b + 1],
                                                   in_=cols["rstd"][:, b:b + 1], func=AF.Exp), inc="aS")
            p.wait("ve", "aS")
            if i >= 1:
                p.wait("ve", "qxs", st_done[i - 1])
            p.do("ve", lambda e, b=b: e.tensor_scalar(out=xnbuf[b][:, :], in0=xbuf[b][:, :],
                                                      scalar1=cols["mean"][:, b:b + 1],
                                                      scalar2=cols["rstd"][:, b:b + 1],
                                                      op0=OP.subtract, op1=OP.mult), inc="vS")
            ln_done.append(p.val("vS"))
            p.wait("po", "vS", ln_done[-1])
            p.dma("po", lambda e, i=i, b=b: e.dma_start(out=xn16[128 * i:128 * (i + 1), :],
                                                        in_=xnbuf[b][:, :]), "qxs")
            st_done.append(p.val("qxs"))
        p.wait("sy", "qxs", st_done[-1])
        for m in range(NK):
            p.dma("sy", lambda e, m=m: e.dma_start(out=xnT[:, T * m:T * (m + 1)],
                                                   in_=xn16[:, 128 * m:128 * (m + 1)], transpose=True), "qxt")
        CNT["xnT"] = p.val("qxt")

    def phase15(p):
        for oh in range(2):
            D2 = D // 2
            if oh == 1:
                p.wait("sy", "qzdst")
            p.dma("sy", lambda e, oh=oh: e.dma_start(
                out=qzd_sb[:, :], in_=E["qz_d"][:, (D // 16) * oh:(D // 16) * (oh + 1)]), "qzd")
            p.dma("sy", lambda e, oh=oh: e.dma_start(
                out=scd_sb[:, :], in_=E["sc_d"][:, D2 * oh:D2 * (oh + 1)]), "qzd")
            p.wait("ve", "qzd")
            CW = D // 16
            for u in range(8):
                p.dof("ve", lambda e, u=u, CW=CW: e.tensor_scalar(
                    out=zqu_sb[:, CW * u:CW * (u + 1)], in0=qzd_sb[:, :],
                    scalar1=GT["shift8"][0:NF, u:u + 1], scalar2=15.0,
                    op0=OP.logical_shift_right, op1=OP.bitwise_and))
            p.dof("ve", lambda e, CW=CW: e.tensor_copy(
                out=zqf_sb[:, :].rearrange("p (c u) -> p c u", u=8),
                in_=zqu_sb[:, :].rearrange("p (u c) -> p c u", u=8)))
            p.dof("ve", lambda e: e.tensor_scalar_add(out=zqf_sb[:, :], in0=zqf_sb[:, :], scalar1=1.0))
            p.dof("ve", lambda e: e.tensor_mul(out=zqf_sb[:, :], in0=zqf_sb[:, :], in1=scd_sb[:, :]))
            p.dof("ve", lambda e: e.tensor_scalar_mul(out=zd16_sb[:, :], in0=zqf_sb[:, :], scalar1=-1.0))
            p.wait("po", "vS")
            p.dma("po", lambda e, oh=oh: e.dma_start(
                out=zd16_h[:, D2 * oh:D2 * (oh + 1)], in_=zd16_sb[:, :]), "qzdst")

        p.barrier()
        wr_done = []
        tt_marks = {}
        bc_marks = {}
        D2 = D // 2
        for step in range(2 * NF):
            m, hh = step // 2, step % 2
            par = step % 2
            if step >= 2:
                p.wait("sy", "vS", tt_marks[step - 2])
            rep = E["qw_d"][16 * m:16 * (m + 1), D2 * hh:D2 * (hh + 1)].rearrange(
                "k (r j) -> k r j", r=1).broadcast_to([16, 8, D2])
            qrepq = "qrep0" if par == 0 else "qrep1"
            qscdq = "qscd0" if par == 0 else "qscd1"
            p.dma("sy", lambda e, rep=rep, par=par: e.dma_start(out=qdbuf[par][:, :], in_=rep), qrepq)
            qldv = p.val(qrepq)
            if step >= 2:
                p.wait("po", "tS", bc_marks[step - 2])
            p.dma("po", lambda e, m=m, hh=hh, par=par: e.dma_start(
                out=scr16d[par][:, :], in_=E["sc_d"][m:m + 1, D2 * hh:D2 * (hh + 1)]), qscdq)
            scldv = p.val(qscdq)
            p.wait("te", qscdq, scldv)
            if step >= 2:
                p.wait("te", "vS", tt_marks[step - 2])
            hs = psA if hh == 0 else psB
            for oc in range(4):
                p.do("te", lambda e, oc=oc, hs=hs, par=par: e.matmul(
                    hs[:, 512 * oc:512 * (oc + 1)],
                    GT["ones1"][:, :], scr16d[par][0:1, 512 * oc:512 * (oc + 1)],
                    start=True, stop=True), inc=("tS" if oc == 3 else None))
            bc_marks[step] = p.val("tS")
            p.wait("ve", qrepq, qldv)
            p.do("ve", lambda e, par=par: e.tensor_scalar(out=qdbuf[par][:, :], in0=qdbuf[par][:, :],
                                                          scalar1=GT["shifts"][:, 0:1], scalar2=15.0,
                                                          op0=OP.logical_shift_right, op1=OP.bitwise_and))
            p.wait("ve", "tS", bc_marks[step])
            if m >= 2 and hh == 0:
                p.wait("ve", "qwst0" if m % 2 == 0 else "qwst1", wr_done[m - 2])
            p.do("ve", lambda e, m=m, hh=hh, par=par, hs=hs: e.tensor_tensor(
                out=wdbuf[m % 2][:, D2 * hh:D2 * (hh + 1)], in0=qdbuf[par][:, :],
                in1=hs[:, :], op=OP.mult), inc="vS")
            tt_marks[step] = p.val("vS")
            if hh == 1:
                p.wait("sy", "vS", tt_marks[step])
                qw = "qwst0" if m % 2 == 0 else "qwst1"
                p.dma("sy", lambda e, m=m: e.dma_start(out=wd16_h[128 * m:128 * (m + 1), :],
                                                       in_=wdbuf[m % 2][:, :]), qw)
                wr_done.append(p.val(qw))
        CNT["wd_done0"] = p.val("qwst0")
        CNT["wd_done1"] = p.val("qwst1")

    def phase2prep(p):
        p.dma("sy", lambda e: e.dma_start(out=selbank_sb[:, :], in_=E["selbank"]), "qsel")
        p.wait("te", "qsel")
        p.wait("te", "qxt", CNT["xnT"])
        p.wait("te", "qc", CNT["consts"])
        p.wait("te", "vS")
        for nch in range(4):
            for m in range(NK):
                p.do("te", lambda e, nch=nch, m=m: e.matmul(
                    psA[0:NK, 512 * nch:512 * (nch + 1)],
                    selbank_sb[:, NK * m:NK * (m + 1)],
                    xnT[:, T * m + 512 * nch: T * m + 512 * (nch + 1)],
                    start=(m == 0), stop=(m == NK - 1)), inc=("tS" if m == NK - 1 else None))
            p.wait("ve", "tS")
            p.dof("ve", lambda e, nch=nch: e.tensor_copy(
                out=GT["sbarT"][:, 512 * nch:512 * (nch + 1)],
                in_=psA[0:NK, 512 * nch:512 * (nch + 1)]))
        CNT["sbar"] = p.val("vS")
        if "d_sbar" in dbg:
            p.wait("po", "vS", CNT["sbar"])
            p.dma("po", lambda e: e.dma_start(out=dbg["d_sbar"], in_=GT["sbarT"][:, :]), "qdbg")

        p.wait("sy", "qcw", CNT["cast_win"])
        p.dma("sy", lambda e: e.dma_start(out=win16_sb[:, :], in_=w16["win"]), "qwin")
        p.wait("te", "qwin")
        p.wait("te", "aS")
        psB16 = psB.bitcast(BF16)
        for m in range(NK):
            if m >= 2:
                p.wait("te", "aS")
            p.do("te", lambda e, m=m: e.transpose(
                psB16[:, 1024 * (m % 2):1024 * (m % 2) + H],
                win16_sb[0:H, 128 * m:128 * (m + 1)], GT["ident16"][0:H, 0:H]), inc="tS")
            p.wait("sc", "tS")
            p.do("sc", lambda e, m=m: e.activation(
                out=winT16[:, H * m:H * (m + 1)],
                in_=psB16[:, 1024 * (m % 2):1024 * (m % 2) + H], func=AF.Identity), inc="aS")
        p.wait("te", "aS")

        for nch in range(4):
            for m in range(NK):
                p.do("te", lambda e, nch=nch, m=m: e.matmul(
                    psA[0:H, 512 * nch:512 * (nch + 1)],
                    winT16[:, H * m:H * (m + 1)],
                    xnT[:, T * m + 512 * nch:T * m + 512 * (nch + 1)],
                    start=(m == 0), stop=(m == NK - 1)), inc=("tS" if m == NK - 1 else None))
        p.wait("sc", "tS")
        p.dof("sc", lambda e: e.activation(out=dtraw[:, :], in_=psA[0:H, 0:T],
                                           func=AF.Identity, bias=GT["dtb"][0:H, 0:1]))
        p.dof("sc", lambda e: e.activation(out=spt1[:, :], in_=dtraw[:, :], func=AF.Abs))
        p.dof("sc", lambda e: e.activation(out=spt1[:, :], in_=spt1[:, :], func=AF.Exp, scale=-1.0))
        p.dof("sc", lambda e: e.activation(out=spt1[:, :], in_=spt1[:, :], func=AF.Ln, bias=1.0))
        p.wait("ve", "aS")
        p.dof("ve", lambda e: e.tensor_scalar_max(out=spt2[:, :], in0=dtraw[:, :], scalar1=0.0))
        p.dof("ve", lambda e: e.tensor_add(out=dtraw[:, :], in0=spt2[:, :], in1=spt1[:, :]))
        p.wait("sc", "vS")
        p.dof("sc", lambda e: e.activation(out=acol[:, :], in_=GT["alog"][:, :], func=AF.Exp))
        p.wait("ve", "aS")
        p.dof("ve", lambda e: e.tensor_scalar_mul(out=acol[:, :], in0=acol[:, :], scalar1=-1.0))
        p.dof("ve", lambda e: e.tensor_scalar_mul(out=spt1[:, :], in0=dtraw[:, :],
                                                  scalar1=acol[0:H, 0:1]))
        for n in range(NT):
            p.dof("ve", lambda e, n=n: e.tensor_tensor_scan(
                out=spt2[:, 128 * n:128 * (n + 1)], data0=spt1[:, 128 * n:128 * (n + 1)],
                data1=spt1[:, 128 * n:128 * (n + 1)], initial=0.0,
                op0=OP.add, op1=OP.min))
        p.wait("te", "vS")
        for n in range(NT):
            p.do("te", lambda e, n=n: e.transpose(
                psB[:, 512 * (n % 2):512 * (n % 2) + H],
                dtraw[0:H, 128 * n:128 * (n + 1)], RT["identf"][0:H, 0:H]), inc="tS")
            p.wait("ve", "tS")
            p.do("ve", lambda e, n=n: e.tensor_copy(
                out=RT["dt_tm"][:, H * n:H * (n + 1)],
                in_=psB[:, 512 * (n % 2):512 * (n % 2) + H]), inc="vS")
            p.wait("te", "vS")
            p.do("te", lambda e, n=n: e.transpose(
                psB[:, 512 * (n % 2):512 * (n % 2) + H],
                spt2[0:H, 128 * n:128 * (n + 1)], RT["identf"][0:H, 0:H]), inc="tS")
            p.wait("ve", "tS")
            p.do("ve", lambda e, n=n: e.tensor_copy(
                out=RT["G_tm"][:, H * n:H * (n + 1)],
                in_=psB[:, 512 * (n % 2):512 * (n % 2) + H]), inc="vS")
        CNT["prep"] = p.val("vS")
        if "d_gtm" in dbg:
            p.wait("po", "vS", CNT["prep"])
            p.dma("po", lambda e: e.dma_start(out=dbg["d_gtm"], in_=RT["G_tm"][:, :]), "qdbg")
            p.dma("po", lambda e: e.dma_start(out=dbg["d_dttm"], in_=RT["dt_tm"][:, :]), "qdbg")

    def proj_rope(p, wsrc, orow, dest):
        for nch in range(4):
            bk = nch % 2
            p.wait("te", "vS")
            for m in range(NK):
                p.do("te", lambda e, nch=nch, bk=bk, m=m: e.matmul(
                    ps_proj[:, 512 * bk:512 * (bk + 1)],
                    WT16[:, 128 * m:128 * (m + 1)],
                    xnT[:, T * m + 512 * nch:T * m + 512 * (nch + 1)],
                    start=(m == 0), stop=(m == NK - 1)), inc=("tS" if m == NK - 1 else None))
            h0 = 512 * nch
            p.wait("sy", "vS")
            p.dma("sy", lambda e, h0=h0: e.dma_start(out=cs_sl[:, :], in_=E["cosT"][:, h0:h0 + 512]), "qcs")
            p.dma("sy", lambda e, h0=h0: e.dma_start(out=sn_sl[:, :], in_=E["sinT"][:, h0:h0 + 512]), "qcs")
            csl = p.val("qcs")
            p.wait("ve", "tS")
            if nch >= 1:
                p.wait("ve", "qrot")
            p.do("ve", lambda e, bk=bk: e.tensor_copy(out=r1[:, :],
                                                      in_=ps_proj[:, 512 * bk:512 * (bk + 1)]), inc="vS")
            rv = p.val("vS")
            p.wait("po", "vS", rv)
            p.dma("po", lambda e: e.dma_start(out=r2[0:64, :], in_=r1[64:128, :]), "qrot")
            p.dma("po", lambda e: e.dma_start(out=r2[64:128, :], in_=r1[0:64, :]), "qrot")
            rotv = p.val("qrot")
            p.wait("ve", "qrot", rotv)
            p.wait("ve", "qcs", csl)
            p.do("ve", lambda e: e.tensor_mul(out=r1[:, :], in0=r1[:, :], in1=cs_sl[:, :]))
            p.do("ve", lambda e: e.tensor_mul(out=r2[:, :], in0=r2[:, :], in1=sn_sl[:, :]))
            p.do("ve", lambda e, h0=h0, dest=dest: e.tensor_add(
                out=dest[:, h0:h0 + 512], in0=r1[:, :], in1=r2[:, :]), inc="vS")

    def load_wslice(p, whT, hb):
        inap = whT[D * hb:D * (hb + 1), :].rearrange("(kt i) o -> i kt o", i=128)
        outap = WT16[:, :].rearrange("p (kt o) -> p kt o", kt=NK)
        p.dma("sy", lambda e, inap=inap, outap=outap: e.dma_start(out=outap, in_=inap), "qwt")
        return p.val("qwt")

    def phase2(p):
        p.wait("po", "vS")
        ytn_marks = []
        for kv in range(KVH):
            p.wait("sy", "qst5", CNT["pt_wk"])
            p.wait("sy", "tS")
            wk_ld = load_wslice(p, wkT_h, kv)
            p.wait("te", "qwt", wk_ld)
            proj_rope(p, w16["wk"], 128 * kv, kT16)
            kv_rope = p.val("vS")
            p.wait("sy", "qst5", CNT["pt_wv"])
            p.wait("sy", "vS", kv_rope)
            p.wait("sy", "tS")
            wv_ld = load_wslice(p, wvT_h, kv)
            p.wait("te", "qwt", wv_ld)
            for tt in range(NT):
                p.wait("te", "vS")
                for m in range(NK):
                    p.do("te", lambda e, tt=tt, m=m, kv=kv: e.matmul(
                        ps_proj[:, 0:128],
                        xnT[:, T * m + 128 * tt:T * m + 128 * (tt + 1)],
                        WT16[:, 128 * m:128 * (m + 1)],
                        start=(m == 0), stop=(m == NK - 1)), inc=("tS" if m == NK - 1 else None))
                p.wait("ve", "tS")
                p.do("ve", lambda e, tt=tt: e.tensor_copy(out=vtm16[:, 128 * tt:128 * (tt + 1)],
                                                          in_=ps_proj[:, 0:128]), inc="vS")
            for n in range(NT):
                p.wait("te", "vS")
                p.do("te", lambda e, n=n: e.transpose(
                    ps_t16[:, :], kT16[:, 128 * n:128 * (n + 1)], GT["ident16"][:, :]), inc="tS")
                p.wait("ve", "tS")
                p.do("ve", lambda e, n=n: e.tensor_copy(out=ktm16[:, 128 * n:128 * (n + 1)],
                                                        in_=ps_t16[:, :]), inc="vS")

            for hj in range(4):
                h = 4 * kv + hj
                p.wait("sy", "qst5", CNT["pt_wq"])
                p.wait("sy", "tS")
                wq_ld = load_wslice(p, wqT_h, h)
                p.wait("te", "qwt", wq_ld)
                proj_rope(p, w16["wq"], 128 * h, qT16)

                gview = RT["G_tm"].rearrange("p (n hh) -> p n hh", hh=H)[:, :, h].rearrange("p n -> p n")
                p.wait("te", "vS")
                p.do("te", lambda e, gview=gview: e.transpose(
                    ps_tf[0:16, 0:128], gview, RT["identf"][:, :]), inc="tS")
                p.wait("sc", "tS")
                p.do("sc", lambda e: e.activation(out=grow_sb[:, :], in_=ps_tf[0:16, 0:128],
                                                  func=AF.Identity), inc="aS")
                p.wait("te", "aS")
                p.do("te", lambda e: e.transpose(ps_tf[0:1, 0:16], grow_sb[:, 127:128],
                                                 RT["identf"][0:16, 0:16]), inc="tS")
                p.wait("sc", "tS")
                p.do("sc", lambda e: e.activation(out=glrow[:, :], in_=ps_tf[0:1, 0:16],
                                                  func=AF.Identity), inc="aS")
                p.wait("te", "aS")
                p.do("te", lambda e: e.matmul(ps_tf[:, 0:16], GT["ones1f"][:, :], glrow[0:1, :],
                                              start=True, stop=True), inc="tS")
                p.wait("sc", "tS")
                p.do("sc", lambda e: e.activation(out=cdall[:, :], in_=ps_tf[:, 0:16],
                                                  func=AF.Exp), inc="aS")
                p.wait("ve", "tS")
                p.do("ve", lambda e, gview=gview: e.tensor_sub(out=fmat[:, :], in0=ps_tf[:, 0:16],
                                                               in1=gview), inc="vS")
                p.wait("sc", "vS")
                p.do("sc", lambda e: e.activation(out=fmat[:, :], in_=fmat[:, :], func=AF.Exp), inc="aS")
                fcd = p.val("aS")

                for n in range(NT):
                    dtcol = RT["dt_tm"][:, H * n + h:H * n + h + 1]
                    gcol = RT["G_tm"][:, H * n + h:H * n + h + 1]
                    p.wait("te", "vS")
                    p.do("te", lambda e, n=n: e.matmul(
                        ps_gb[:, :], RT["eb16"][:, 128 * n:128 * (n + 1)], grow_sb[:, :],
                        start=True, stop=True), inc="tS")
                    p.wait("ve", "tS")
                    p.do("ve", lambda e, gcol=gcol: e.scalar_tensor_tensor(
                        out=dec_scr[:, :], in0=ps_gb[:, :], scalar=gcol, in1=RT["maskT"][:, :],
                        op0=OP.subtract, op1=OP.add), inc="vS")
                    p.wait("sc", "vS")
                    p.wait("sc", "aS", fcd)
                    p.do("sc", lambda e: e.activation(out=expg16[:, :], in_=dec_scr[:, :],
                                                      func=AF.Exp), inc="aS")
                    p.wait("te", "vS")
                    p.do("te", lambda e, n=n: e.matmul(
                        ps_sT[:, :], kT16[:, 128 * n:128 * (n + 1)], qT16[:, 128 * n:128 * (n + 1)],
                        start=True, stop=True), inc="tS")
                    p.wait("ve", "tS")
                    p.wait("ve", "aS")
                    p.do("ve", lambda e, n=n: e.tensor_tensor(
                        out=sdec_all[:, 128 * n:128 * (n + 1)], in0=ps_sT[:, :], in1=expg16[:, :],
                        op=OP.mult))
                    p.do("ve", lambda e, n=n, dtcol=dtcol: e.tensor_scalar_mul(
                        out=xc16[:, :], in0=vtm16[:, 128 * n:128 * (n + 1)], scalar1=dtcol))
                    p.do("ve", lambda e, n=n: e.tensor_scalar_mul(
                        out=kdec16[:, :], in0=ktm16[:, 128 * n:128 * (n + 1)],
                        scalar1=fmat[:, n:n + 1]), inc="vS")
                    p.wait("te", "vS")
                    p.do("te", lambda e: e.matmul(ps_cs[:, :], kdec16[:, :], xc16[:, :],
                                                  start=True, stop=True), inc="tS")
                    p.wait("ve", "tS")
                    p.do("ve", lambda e, n=n: e.tensor_copy(out=S_all[:, 128 * n:128 * (n + 1)],
                                                            in_=ps_cs[:, :]), inc="vS")

                for n in range(1, NT):
                    s_prev = S_all[:, 128 * (n - 1):128 * n]
                    s_cur = S_all[:, 128 * n:128 * (n + 1)]
                    p.do("ve", lambda e, s_prev=s_prev, s_cur=s_cur, n=n: e.scalar_tensor_tensor(
                        out=s_cur, in0=s_prev, scalar=cdall[:, n:n + 1], in1=s_cur,
                        op0=OP.mult, op1=OP.add), inc="vS")

                for n in range(NT):
                    dtcol = RT["dt_tm"][:, H * n + h:H * n + h + 1]
                    p.wait("te", "vS")
                    p.wait("te", "aS")
                    p.do("te", lambda e, n=n: e.matmul(
                        ps_gb[:, :], RT["eb16"][:, 128 * n:128 * (n + 1)], grow_sb[:, :],
                        start=True, stop=True), inc="tS")
                    p.wait("sc", "tS")
                    p.do("sc", lambda e: e.activation(out=expg16[:, :], in_=ps_gb[:, :],
                                                      func=AF.Exp), inc="aS")
                    p.wait("po", "aS")
                    p.wait("po", "tS")
                    p.do("po", lambda e, n=n: e.tensor_tensor(
                        out=expg16[:, :], in0=qT16[:, 128 * n:128 * (n + 1)],
                        in1=expg16[:, :], op=OP.mult))
                    p.do("po", lambda e, n=n, dtcol=dtcol: e.tensor_scalar_mul(
                        out=xc16[:, :], in0=vtm16[:, 128 * n:128 * (n + 1)], scalar1=dtcol), inc="poS")
                    if n >= 1:
                        s_prev = S_all[:, 128 * (n - 1):128 * n]
                        p.do("ve", lambda e, s_prev=s_prev: e.tensor_copy(out=s16c[:, :], in_=s_prev), inc="vS")
                        p.wait("te", "vS")
                    p.wait("te", "poS")
                    if n >= 1:
                        p.do("te", lambda e: e.matmul(ps_y[:, :], s16c[:, :], expg16[:, :],
                                                      start=True, stop=False))
                    p.do("te", lambda e, n=n: e.matmul(
                        ps_y[:, :], xc16[:, :], sdec_all[:, 128 * n:128 * (n + 1)],
                        start=(n == 0), stop=True), inc="tS")
                    p.wait("ve", "tS")
                    p.do("ve", lambda e, n=n: e.tensor_copy(out=y16[:, 128 * n:128 * (n + 1)],
                                                            in_=ps_y[:, :]), inc="vS")
                    p.wait("sc", "vS")
                    p.do("sc", lambda e: e.activation(out=sq_scr[:, :], in_=ps_y[:, :],
                                                      func=AF.Square), inc="aS")
                    p.wait("te", "aS")
                    p.do("te", lambda e: e.matmul(ps_tf[0:1, 0:128], GT["ones128f"][:, :],
                                                  sq_scr[:, :], start=True, stop=True), inc="tS")
                    p.wait("ve", "tS")
                    p.do("ve", lambda e, n=n: e.tensor_copy(
                        out=ssum[0:1, 128 * n:128 * (n + 1)], in_=ps_tf[0:1, 0:128]), inc="vS")

                p.do("ve", lambda e: e.tensor_scalar(out=rinv[:, :], in0=ssum[:, :],
                                                     scalar1=1.0 / 128.0, scalar2=EPS,
                                                     op0=OP.mult, op1=OP.add), inc="vS")
                p.wait("sc", "vS")
                p.do("sc", lambda e: e.activation(out=rinv[:, :], in_=rinv[:, :], func=AF.Ln), inc="aS")
                p.wait("ve", "aS")
                p.do("ve", lambda e: e.tensor_scalar_mul(out=rinv[:, :], in0=rinv[:, :],
                                                         scalar1=-0.5), inc="vS")
                p.wait("sc", "vS")
                p.do("sc", lambda e: e.activation(out=rinv[:, :], in_=rinv[:, :], func=AF.Exp), inc="aS")
                rinv_done = p.val("aS")

                p.wait("sy", "qst5", CNT["pt_wg"])
                p.wait("sy", "tS")
                wg_ld = load_wslice(p, wgT_h, h)
                p.wait("te", "qwt", wg_ld)
                for nch in range(4):
                    p.wait("te", "vS")
                    for m in range(NK):
                        p.do("te", lambda e, nch=nch, m=m: e.matmul(
                            ps_proj[:, 512 * (nch % 2):512 * (nch % 2 + 1)],
                            WT16[:, 128 * m:128 * (m + 1)],
                            xnT[:, T * m + 512 * nch:T * m + 512 * (nch + 1)],
                            start=(m == 0), stop=(m == NK - 1)), inc=("tS" if m == NK - 1 else None))
                    p.wait("sc", "tS")
                    p.wait("sc", "aS", rinv_done)
                    p.do("sc", lambda e, nch=nch: e.activation(
                        out=gatechunk[:, :], in_=ps_proj[:, 512 * (nch % 2):512 * (nch % 2 + 1)],
                        func=AF.Silu), inc="aS")
                    for s4 in range(4):
                        tpos = 512 * nch + 128 * s4
                        p.wait("te", "aS")
                        p.wait("te", "vS")
                        p.do("te", lambda e, tpos=tpos: e.matmul(
                            ps_gb[:, :], GT["ones1f"][:, :], rinv[0:1, tpos:tpos + 128],
                            start=True, stop=True), inc="tS")
                        p.wait("ve", "tS")
                        p.wait("ve", "aS")
                        p.do("ve", lambda e, tpos=tpos: e.tensor_tensor(
                            out=dec_scr[:, :], in0=ps_gb[:, :], in1=y16[:, tpos:tpos + 128],
                            op=OP.mult))
                        p.do("ve", lambda e, s4=s4: e.scalar_tensor_tensor(
                            out=ynst[:, 128 * s4:128 * (s4 + 1)], in0=dec_scr[:, :],
                            scalar=GT["gnw"][:, 0:1], in1=gatechunk[:, 128 * s4:128 * (s4 + 1)],
                            op0=OP.mult, op1=OP.mult), inc="vS")
                    ynv = p.val("vS")
                    p.wait("po", "vS", ynv)
                    p.dma("po", lambda e, h=h, nch=nch: e.dma_start(
                        out=ytn_h[128 * h:128 * (h + 1), 512 * nch:512 * (nch + 1)],
                        in_=ynst[:, :]), "qyt")
                    p.wait("ve", "qyt")
        CNT["ytn"] = p.val("qyt")

    def phase2b(p):
        p.wait("sy", "qyt", CNT["ytn"])
        st_marks = []
        for tq in range(4):
            p.wait("sy", "tS")
            for m in range(NK):
                p.dma("sy", lambda e, m=m, tq=tq: e.dma_start(
                    out=ytq[m][:, :], in_=ytn_h[128 * m:128 * (m + 1), 512 * tq:512 * (tq + 1)]), "qyq")
            yld = p.val("qyq")
            p.wait("te", "qyq", yld)
            for oc in range(8):
                p.wait("sy", "qst5", CNT["pt_wo"])
                kgmarks = []
                for kg in range(4):
                    if kg >= 2:
                        p.wait("sy", "tS", kgmarks[kg - 2])
                    qw2 = "qw2a" if kg % 2 == 0 else "qw2b"
                    inap = woT_h[1024 * kg:1024 * (kg + 1), 512 * oc:512 * (oc + 1)].rearrange(
                        "(k i) o -> i k o", i=128)
                    outap = WT2[kg % 2][:, :].rearrange("p (k o) -> p k o", k=8)
                    p.dma("sy", lambda e, inap=inap, outap=outap: e.dma_start(
                        out=outap, in_=inap), qw2)
                    wld = p.val(qw2)
                    p.wait("te", qw2, wld)
                    if kg == 0 and (oc > 0 or tq > 0):
                        p.wait("te", "vS")
                    for k in range(8):
                        kt = 8 * kg + k
                        for tt in range(4):
                            p.do("te", lambda e, kt=kt, k=k, kg=kg, tt=tt: e.matmul(
                                psD[:, 512 * tt:512 * (tt + 1)],
                                ytq[kt][:, 128 * tt:128 * (tt + 1)],
                                WT2[kg % 2][:, 512 * k:512 * (k + 1)],
                                start=(kg == 0 and k == 0), stop=(kg == 3 and k == 7)),
                                inc=("tS" if (k == 7 and tt == 3) else None))
                    kgmarks.append(p.val("tS"))
                mmv = p.val("tS")
                for tt in range(4):
                    trow = 512 * tq + 128 * tt
                    qxr = "qxr0" if tt % 2 == 0 else "qxr1"
                    qo = "qo0" if tt % 2 == 0 else "qo1"
                    if len(st_marks) >= 2:
                        p.wait("sy", "qo0" if st_marks[-2][0] == 0 else "qo1", st_marks[-2][1])
                    p.dma("sy", lambda e, trow=trow, oc=oc, tt=tt: e.dma_start(
                        out=xrd[tt % 2][:, :],
                        in_=E["x"][trow:trow + 128, 512 * oc:512 * (oc + 1)]), qxr)
                    xldv = p.val(qxr)
                    p.wait("ve", "tS", mmv)
                    p.wait("ve", qxr, xldv)
                    p.wait("ve", qo)
                    p.do("ve", lambda e, tt=tt: e.tensor_add(
                        out=outw[tt % 2][:, :],
                        in0=psD[:, 512 * tt:512 * (tt + 1)], in1=xrd[tt % 2][:, :]), inc="vS")
                    owv = p.val("vS")
                    p.wait("po", "vS", owv)
                    p.dma("po", lambda e, trow=trow, oc=oc, tt=tt: e.dma_start(
                        out=out[trow:trow + 128, 512 * oc:512 * (oc + 1)],
                        in_=outw[tt % 2][:, :]), qo)
                    st_marks.append((tt % 2, p.val(qo)))
        CNT["out_attn0"] = p.val("qo0")
        CNT["out_attn1"] = p.val("qo1")

    def phase3(p):
        st3 = {"tt": {}, "ttg": [], "bcg": []}

        def dequant_side(p, c, side):
            jbase = 256 * c + (0 if side == "g" else F)
            W1buf = W1g if side == "g" else W1u
            key = (c, side)
            p.wait("sy", "vS")
            p.dma("sy", lambda e, jbase=jbase: e.dma_start(
                out=zpk[:, :], in_=E["qz_gu"][:, jbase // 8:jbase // 8 + 32]), "qz3")
            p.dma("sy", lambda e, jbase=jbase: e.dma_start(
                out=zs[:, :], in_=E["sc_gu"][0:32, jbase:jbase + 256]), "qz3")
            zld = p.val("qz3")
            p.wait("ve", "qz3", zld)
            p.wait("ve", "tS")
            for u in range(8):
                p.dof("ve", lambda e, u=u: e.tensor_scalar(
                    out=zun[:, 32 * u:32 * (u + 1)], in0=zpk[:, :],
                    scalar1=GT["shift8"][0:32, u:u + 1], scalar2=15.0,
                    op0=OP.logical_shift_right, op1=OP.bitwise_and))
            p.dof("ve", lambda e: e.tensor_copy(
                out=zf[:, :].rearrange("p (cc u) -> p cc u", u=8),
                in_=zun[:, :].rearrange("p (u cc) -> p cc u", u=8)))
            p.dof("ve", lambda e: e.tensor_scalar_add(out=zf[:, :], in0=zf[:, :], scalar1=1.0))
            p.dof("ve", lambda e: e.tensor_mul(out=zf[:, :], in0=zf[:, :], in1=zs[:, :]))
            p.dof("ve", lambda e, side=side: e.tensor_scalar_mul(
                out=zb16[side][:, :], in0=zf[:, :], scalar1=-1.0))
            st3["tt"][(key, "z")] = p.val("vS")

            for h8 in range(8):
                gi = len(st3["ttg"])
                par = h8 % 2
                if gi >= 2:
                    p.wait("sy", "vS", st3["ttg"][gi - 2])
                qr3q = "qr30" if par == 0 else "qr31"
                qsc3q = "qsc30" if par == 0 else "qsc31"
                for ktl in range(4):
                    kt = 4 * h8 + ktl
                    r0 = 16 * kt
                    rep = E["qw_gu"][r0:r0 + 16, jbase:jbase + 256].rearrange(
                        "k (r j) -> k r j", r=1).broadcast_to([16, 8, 256])
                    p.dma("sy", lambda e, rep=rep, ktl=ktl, par=par: e.dma_start(
                        out=qbuf[par][:, 256 * ktl:256 * (ktl + 1)], in_=rep), qr3q)
                qldv = p.val(qr3q)
                screin = E["sc_gu"][4 * h8:4 * h8 + 4, jbase:jbase + 256]
                if gi >= 2:
                    p.wait("po", "tS", st3["bcg"][gi - 2])
                p.dma("po", lambda e, screin=screin, par=par: e.dma_start(
                    out=scrow3[par][:, :].rearrange("o (kt j) -> o kt j", kt=4),
                    in_=screin.rearrange("(o kt) j -> o kt j", o=1)), qsc3q)
                sc16v = p.val(qsc3q)
                p.wait("te", qsc3q, sc16v)
                if gi >= 2:
                    p.wait("te", "vS", st3["ttg"][gi - 2])
                else:
                    p.wait("te", "vS")
                for u in range(2):
                    p.do("te", lambda e, u=u, par=par: e.matmul(
                        psA[:, 1024 * par + 512 * u:1024 * par + 512 * (u + 1)],
                        GT["ones1"][:, :], scrow3[par][0:1, 512 * u:512 * (u + 1)],
                        start=True, stop=True), inc=("tS" if u == 1 else None))
                scv = p.val("tS")
                st3["bcg"].append(scv)
                p.wait("ve", qr3q, qldv)
                p.do("ve", lambda e, par=par: e.tensor_scalar(
                    out=qbuf[par][:, :], in0=qbuf[par][:, :],
                    scalar1=GT["shifts"][:, 0:1], scalar2=15.0,
                    op0=OP.logical_shift_right, op1=OP.bitwise_and))
                p.wait("ve", "tS", scv)
                p.do("ve", lambda e, W1buf=W1buf, h8=h8, par=par: e.tensor_tensor(
                    out=W1buf[:, 1024 * h8:1024 * (h8 + 1)], in0=qbuf[par][:, :],
                    in1=psA[:, 1024 * par:1024 * (par + 1)], op=OP.mult), inc="vS")
                st3["tt"][(key, h8)] = p.val("vS")
                st3["ttg"].append(p.val("vS"))

        def mm_side(p, c, side):
            W1buf = W1g if side == "g" else W1u
            key = (c, side)
            if "d_w21" in dbg and c == DBG_C and side == "g":
                p.wait("po", "vS", st3["tt"][(key, "z")])
                p.dma("po", lambda e: e.dma_start(out=dbg["d_w21"], in_=W1g[:, 0:8192]), "qdbg")
                p.dma("po", lambda e: e.dma_start(out=dbg["d_z21"], in_=zb16["g"][:, :]), "qdbg")
                p.dma("po", lambda e: e.dma_start(out=dbg["d_w21u"], in_=W1u[:, 0:8192]), "qdbg")
            p.wait("te", "vS", st3["tt"][(key, "z")])
            for mt in range(2):
                if side == "g":
                    p.wait("te", "aS")
                    p.wait("te", "vS")
                else:
                    p.wait("te", "aS")
                    p.wait("te", "vS")
                for nch in range(4):
                    for kt in range(33):
                        if kt < 32:
                            lhs = W1buf[:, 256 * kt + 128 * mt:256 * kt + 128 * (mt + 1)]
                            rhs = xnT[:, T * kt + 512 * nch:T * kt + 512 * (nch + 1)]
                        else:
                            lhs = zb16[side][:, 128 * mt:128 * (mt + 1)]
                            rhs = GT["sbarT"][:, 512 * nch:512 * (nch + 1)]
                        p.do("te", lambda e, lhs=lhs, rhs=rhs, nch=nch, kt=kt: e.matmul(
                            psB[:, 512 * nch:512 * (nch + 1)], lhs, rhs,
                            start=(kt == 0), stop=(kt == 32)),
                            inc=("tS" if (kt == 32 and nch == 3) else None))
                mmv = p.val("tS")
                if side == "g":
                    p.wait("sc", "tS", mmv)
                    for nch in range(4):
                        p.do("sc", lambda e, mt=mt, nch=nch: e.activation(
                            out=gstash[mt][:, 512 * nch:512 * (nch + 1)],
                            in_=psB[:, 512 * nch:512 * (nch + 1)], func=AF.Silu),
                            inc=("aS" if nch == 3 else None))
                else:
                    p.wait("ve", "tS", mmv)
                    p.wait("ve", "qhst")
                    for nch in range(4):
                        p.do("ve", lambda e, mt=mt, nch=nch: e.tensor_tensor(
                            out=h16sb[:, 512 * nch:512 * (nch + 1)],
                            in0=gstash[mt][:, 512 * nch:512 * (nch + 1)],
                            in1=psB[:, 512 * nch:512 * (nch + 1)], op=OP.mult),
                            inc=("vS" if nch == 3 else None))
                    hv = p.val("vS")
                    fidx = 2 * c + mt
                    p.wait("sy", "vS", hv)
                    p.dma("sy", lambda e, fidx=fidx: e.dma_start(
                        out=hT_h[128 * fidx:128 * (fidx + 1), :], in_=h16sb[:, :]), "qhst")
                    p.wait("te", "vS", hv)
                    for nch in range(4):
                        p.do("te", lambda e, nch=nch: e.matmul(
                            psB[0:1, 512 * nch:512 * (nch + 1)], GT["ones128"][:, :],
                            h16sb[:, 512 * nch:512 * (nch + 1)],
                            start=True, stop=True), inc=("tS" if nch == 3 else None))
                    p.wait("ve", "tS")
                    p.wait("ve", "qhbr")
                    p.dof("ve", lambda e: e.tensor_copy(out=hrow[:, :], in_=psB[0:1, 0:T]))
                    hrv = p.val("vS")
                    p.wait("sy", "vS", hrv)
                    p.dma("sy", lambda e, fidx=fidx: e.dma_start(
                        out=hbar_h[fidx:fidx + 1, :], in_=hrow[:, :]), "qhbr")

        dequant_side(p, 0, "g")
        dequant_side(p, 0, "u")
        for c in range(43):
            mm_side(p, c, "g")
            if c + 1 < 43:
                dequant_side(p, c + 1, "g")
            mm_side(p, c, "u")
            if c + 1 < 43:
                dequant_side(p, c + 1, "u")
        CNT["hT"] = p.val("qhst")
        CNT["hbar"] = p.val("qhbr")

    def phase4(p):
        p.wait("sy", "qwst0", CNT["wd_done0"])
        p.wait("sy", "qwst1", CNT["wd_done1"])
        p.wait("sy", "qhst", CNT["hT"])
        p.wait("sy", "qhbr", CNT["hbar"])
        p.dma("sy", lambda e: e.dma_start(out=hbarL[:, :], in_=hbar_h), "qhbl")
        st_marks = []
        for th in range(2):
            p.wait("sy", "tS")
            for kt in range(NF):
                p.dma("sy", lambda e, kt=kt, th=th: e.dma_start(
                    out=hTh[:, 1024 * kt:1024 * (kt + 1)],
                    in_=hT_h[128 * kt:128 * (kt + 1), 1024 * th:1024 * (th + 1)]), "qhT")
            hld = p.val("qhT")
            for oc in range(8):
                p.wait("sy", "tS")
                p.wait("sy", "qzdst", p.val("qzdst"))
                p.dma("sy", lambda e, oc=oc: e.dma_start(
                    out=zdsl[:, :], in_=zd16_h[:, 512 * oc:512 * (oc + 1)]), "qzs4")
                wd_marks = []
                mm_marks = []
                qwdq = ["qwd0", "qwd1", "qwd2"]
                for pre in range(2):
                    p.dma("sy", lambda e, oc=oc, pre=pre: e.dma_start(
                        out=wdt[pre][:, :], in_=wd16_h[128 * pre:128 * (pre + 1),
                                                       512 * oc:512 * (oc + 1)]), qwdq[pre])
                    wd_marks.append((qwdq[pre], p.val(qwdq[pre])))
                for kt in range(NF + 1):
                    if kt + 2 < NF:
                        if kt + 2 >= 3:
                            p.wait("sy", "tS", mm_marks[kt - 1] if kt - 1 < len(mm_marks) else p.val("tS"))
                        qwd = qwdq[(kt + 2) % 3]
                        p.dma("sy", lambda e, kt=kt, oc=oc: e.dma_start(
                            out=wdt[(kt + 2) % 3][:, :],
                            in_=wd16_h[128 * (kt + 2):128 * (kt + 3), 512 * oc:512 * (oc + 1)]), qwd)
                        wd_marks.append((qwd, p.val(qwd)))
                    if kt < NF:
                        p.wait("te", wd_marks[kt][0], wd_marks[kt][1])
                    else:
                        p.wait("te", "qzs4")
                        p.wait("te", "qhbl")
                        p.wait("te", "qhT", hld)
                    if kt == 0 and (oc > 0 or th > 0):
                        p.wait("te", "vS")
                    for tt in range(8):
                        if kt < NF:
                            lhs = hTh[:, 1024 * kt + 128 * tt:1024 * kt + 128 * (tt + 1)]
                            rhs = wdt[kt % 3][:, :]
                        else:
                            lhs = hbarL[:, 1024 * th + 128 * tt:1024 * th + 128 * (tt + 1)]
                            rhs = zdsl[:, :]
                        p.do("te", lambda e, lhs=lhs, rhs=rhs, tt=tt, kt=kt: e.matmul(
                            psD[:, 512 * tt:512 * (tt + 1)], lhs, rhs,
                            start=(kt == 0), stop=(kt == NF)),
                            inc=("tS" if tt == 7 else None))
                    mm_marks.append(p.val("tS"))
                mmv = p.val("tS")
                for tt in range(8):
                    trow = 1024 * th + 128 * tt
                    qor = "qor0" if tt % 2 == 0 else "qor1"
                    qo4 = "qo4a" if tt % 2 == 0 else "qo4b"
                    if len(st_marks) >= 2:
                        p.wait("sy", st_marks[-2][0], st_marks[-2][1])
                    rsrc = out if DO_ATTN else E["x"]
                    if DO_ATTN:
                        p.wait("sy", "qo0", CNT["out_attn0"])
                        p.wait("sy", "qo1", CNT["out_attn1"])
                    p.dma("sy", lambda e, trow=trow, oc=oc, tt=tt, rsrc=rsrc: e.dma_start(
                        out=ord_[tt % 2][:, :], in_=rsrc[trow:trow + 128, 512 * oc:512 * (oc + 1)]), qor)
                    xldv = p.val(qor)
                    p.wait("ve", "tS", mmv)
                    p.wait("ve", qor, xldv)
                    p.wait("ve", qo4)
                    p.do("ve", lambda e, tt=tt: e.tensor_add(
                        out=owr_[tt % 2][:, :], in0=psD[:, 512 * tt:512 * (tt + 1)],
                        in1=ord_[tt % 2][:, :]), inc="vS")
                    owv = p.val("vS")
                    p.wait("po", "vS", owv)
                    p.dma("po", lambda e, trow=trow, oc=oc, tt=tt: e.dma_start(
                        out=out[trow:trow + 128, 512 * oc:512 * (oc + 1)],
                        in_=owr_[tt % 2][:, :]), qo4)
                    st_marks.append((qo4, p.val(qo4)))

    def build(p):
        for nm, src, dst in [("ident16", E["ident16"], GT["ident16"]),
                             ("identf", E["identf"], RT["identf"]),
                             ("maskT", E["maskT"], RT["maskT"]),
                             ("eb16", E["eb16"], RT["eb16"]),
                             ("shifts", E["shifts"], GT["shifts"]),
                             ("shift8", E["shift8"], GT["shift8"]),
                             ("ones1", E["ones1"], GT["ones1"]),
                             ("ones1f", E["ones1f"], GT["ones1f"]),
                             ("ones128", E["ones128"], GT["ones128"]),
                             ("ones128f", E["ones128f"], GT["ones128f"]),
                             ("gnw", E["gnw"], GT["gnw"]),
                             ("dtb", E["dtb"], GT["dtb"]),
                             ("alog", E["alog"], GT["alog"])]:
            p.dma("sy", lambda e, dst=dst, src=src: e.dma_start(out=dst[:, :], in_=src), "qc")
        CNT["consts"] = p.val("qc")

        castq = {"wk": "qck", "wv": "qcv", "win": "qcw", "wq": "qcq", "wg": "qcg", "wo": "qco"}
        for wname in ["wk", "wv", "win", "wq", "wg", "wo"]:
            w32 = E[wname]
            wb = w16[wname]
            rows = w32.shape[0]
            nchunk = max(1, rows // 512)
            step = rows // nchunk
            for cidx in range(nchunk):
                p.dma("po", lambda e, a=w32, b=wb, r0=cidx * step, r1=(cidx + 1) * step:
                      e.dma_start(out=b[r0:r1, :], in_=a[r0:r1, :]), castq[wname])
            CNT["cast_" + wname] = p.val(castq[wname])

        phase1(p)
        phase05(p)
        p.barrier()
        if "d_xnT" in dbg:
            p.wait("po", "qxt", CNT["xnT"])
            p.dma("po", lambda e: e.dma_start(out=dbg["d_xnT"], in_=xnT[:, 0:NK * T]), "qdbg")
        if DO_FFN:
            phase15(p)
            p.barrier()
        if "d_xnT2" in dbg:
            p.wait("po", "vS")
            p.dma("po", lambda e: e.dma_start(out=dbg["d_xnT2"], in_=xnT[:, 0:NK * T]), "qdbg")
        phase2prep(p)
        p.barrier()
        if DO_ATTN:
            phase2(p)
            p.barrier()
            phase2b(p)
            p.barrier()
        if DO_FFN:
            phase3(p)
            p.barrier()
            phase4(p)
        p.barrier()
        if "d_hbar" in dbg:
            for mm in range(1):
                p.dma("po", lambda e: e.dma_start(out=dbg["d_hbar"], in_=hbar_h), "qdbg")
            p.dma("po", lambda e: e.dma_start(out=dbg["d_zd"], in_=zd16_h), "qdbg")
        if "d_ytn" in dbg:
            for mm in range(NK):
                p.dma("po", lambda e, mm=mm: e.dma_start(out=dbg["d_ytn"][128 * mm:128 * (mm + 1), :],
                                                         in_=ytn_h[128 * mm:128 * (mm + 1), :]), "qdbg")
        if "d_wd" in dbg:
            for mm in range(NF):
                p.dma("po", lambda e, mm=mm: e.dma_start(out=dbg["d_wd"][128 * mm:128 * (mm + 1), :],
                                                         in_=wd16_h[128 * mm:128 * (mm + 1), :]), "qdbg")
        if "d_ht" in dbg:
            for mm in range(NF):
                p.dma("po", lambda e, mm=mm: e.dma_start(out=dbg["d_ht"][128 * mm:128 * (mm + 1), :],
                                                         in_=hT_h[128 * mm:128 * (mm + 1), :]), "qdbg")
        for s in ["qo4a", "qo4b", "qo0", "qo1", "qdbg", "qxs"]:
            p.wait("po", s, p.val(s))

    with nc.Block() as block:
        @block.sync
        def _(eng):
            p.run("sy", eng, build)

        @block.tensor
        def _(eng):
            p.run("te", eng, build)

        @block.vector
        def _(eng):
            p.run("ve", eng, build)

        @block.scalar
        def _(eng):
            p.run("sc", eng, build)

        @block.gpsimd
        def _(eng):
            p.run("po", eng, build)

    return nc



def make_base(inputs):
    consts = _host_constants()
    base = {
        "L0_attn_Wq": inputs["L0_attn_Wq"],
        "L0_attn_Wk": inputs["L0_attn_Wk"],
        "L0_attn_Wv": inputs["L0_attn_Wv"],
        "L0_attn_Wo": inputs["L0_attn_Wo"],
        "L0_attn_Wg": inputs["L0_attn_Wg"],
        "L0_attn_Win": inputs["L0_attn_Win"],
        "dtb": np.ascontiguousarray(inputs["L0_attn_dt_bias"], dtype=np.float32).reshape(H, 1),
        "alog": np.ascontiguousarray(inputs["L0_attn_A_log_bias"], dtype=np.float32).reshape(H, 1),
        "gnw": np.ascontiguousarray(inputs["L0_attn_gnorm_w"], dtype=np.float32).reshape(128, 1),
        "L0_ffn_qweight_gu": inputs["L0_ffn_qweight_gu"],
        "L0_ffn_qzeros_gu": inputs["L0_ffn_qzeros_gu"],
        "L0_ffn_scales_gu": inputs["L0_ffn_scales_gu"],
        "L0_ffn_qweight_down": inputs["L0_ffn_qweight_down"],
        "L0_ffn_qzeros_down": inputs["L0_ffn_qzeros_down"],
        "L0_ffn_scales_down": inputs["L0_ffn_scales_down"],
    }
    base.update(consts)
    return base


def kernel(**inputs):
    x = np.ascontiguousarray(inputs["L0_attn_hidden_states"])
    base = make_base(inputs)
    nc = build_program()
    in_maps = [dict(base, x=np.ascontiguousarray(x[c])) for c in range(8)]
    res = run_bass_kernel_spmd(nc, in_maps, core_ids=list(range(8)))
    return np.stack([r["out"] for r in res.results], axis=0)


# version 04 (final)
# speedup vs baseline: 1.0897x; 1.0897x over previous
import contextlib
import numpy as np
import ml_dtypes
import concourse.bass as bass
import concourse.mybir as mybir
import concourse.bass_utils as _bu
from concourse.bass_utils import run_bass_kernel_spmd
from concourse.alu_op_type import AluOpType as OP

F32 = mybir.dt.float32
BF16 = mybir.dt.bfloat16
U32 = mybir.dt.uint32
AF = mybir.ActivationFunctionType

_orig_run_command = _bu.run_command


def _patched_run_command(cmd, *a, **k):
    if isinstance(cmd, list):
        cmd = ["--enable-birsim=false" if c == "--enable-birsim=true" else c for c in cmd]
    return _orig_run_command(cmd, *a, **k)


_bu.run_command = _patched_run_command

T = 2048
NT = 16
D = 4096
NK = 32
H = 32
KVH = 8
HD = 128
F = 11008
NF = 86
GU = 2 * F
EPS = 1e-5

DO_ATTN = True
DO_FFN = True
DBG_C = 21


class Prog:

    def __init__(self, sems):
        self.semnames = list(sems)
        self.sems = sems
        self.eng = None
        self.engobj = None
        self.cnt = {}

    def run(self, engname, engobj, build):
        self.eng = engname
        self.engobj = engobj
        self.cnt = {s: 0 for s in self.semnames}
        build(self)

    def do(self, engname, fn, inc=None, inc_n=1):
        if inc is not None:
            self.cnt[inc] += inc_n
        if self.eng == engname:
            inst = fn(self.engobj)
            if inc is not None and inst is not None:
                inst.then_inc(self.sems[inc], inc_n)

    def dma(self, engname, fn, q):
        self.do(engname, fn, inc=q, inc_n=16)

    def wait(self, engname, semname, val=None):
        v = self.cnt[semname] if val is None else val
        if self.eng == engname:
            self.engobj.wait_ge(self.sems[semname], v)
        return v

    def val(self, semname):
        return self.cnt[semname]

    OWN = {"ve": "vS", "sc": "aS", "te": "tS", "po": "poS"}

    def dof(self, engname, fn):
        own = Prog.OWN[engname]
        self.do(engname, fn, inc=own)
        self.wait(engname, own)

    def barrier(self):
        for e in ["sy", "te", "ve", "sc", "po"]:
            for s in self.semnames:
                self.wait(e, s)



def _host_constants():
    c = {}
    c["c_ident16"] = np.eye(128, dtype=ml_dtypes.bfloat16)
    c["c_identf"] = np.eye(128, dtype=np.float32)
    inv_freq = (1.0 / 10000.0 ** (np.arange(0, HD, 2, dtype=np.float64) / HD))
    ang = np.arange(T, dtype=np.float64)[:, None] * inv_freq[None, :]
    cos = np.cos(ang)
    sin = np.sin(ang)
    c["c_cosT"] = np.concatenate([cos, cos], axis=1).T.astype(ml_dtypes.bfloat16)
    c["c_sinT"] = np.concatenate([-sin, sin], axis=1).T.astype(ml_dtypes.bfloat16)
    jj = np.arange(128)[:, None]
    ii = np.arange(128)[None, :]
    c["c_maskT"] = np.where(ii >= jj, 0.0, -1e30).astype(np.float32)
    c["c_shifts"] = (4 * (np.arange(128, dtype=np.uint32) % 8)).reshape(128, 1)
    c["c_shift8"] = np.tile((4 * np.arange(8, dtype=np.uint32))[None, :], (128, 1))
    c["c_ones1"] = np.ones((1, 128), dtype=ml_dtypes.bfloat16)
    c["c_ones1f"] = np.ones((1, 128), dtype=np.float32)
    c["c_ones128"] = np.ones((128, 1), dtype=ml_dtypes.bfloat16)
    c["c_ones128f"] = np.ones((128, 1), dtype=np.float32)
    sel = np.zeros((128, NK * NK), dtype=ml_dtypes.bfloat16)
    for m in range(NK):
        sel[:, NK * m + m] = 1.0
    c["c_selbank"] = sel
    eb = np.zeros((16, 16 * 128), dtype=np.float32)
    for n in range(16):
        eb[n, 128 * n:128 * (n + 1)] = 1.0
    c["c_eb16"] = eb
    return c



def build_program(debug=()):
    nc = bass.Bass("TRN2", target_bir_lowering=False)

    def din(name, shape, dt):
        return nc.dram_tensor(name, shape, dt, kind="ExternalInput").ap()

    E = {}
    E["x"] = din("x", [T, D], F32)
    E["wq"] = din("L0_attn_Wq", [D, D], F32)
    E["wk"] = din("L0_attn_Wk", [KVH * HD, D], F32)
    E["wv"] = din("L0_attn_Wv", [KVH * HD, D], F32)
    E["wo"] = din("L0_attn_Wo", [D, D], F32)
    E["wg"] = din("L0_attn_Wg", [D, D], F32)
    E["win"] = din("L0_attn_Win", [H, D], F32)
    E["dtb"] = din("dtb", [H, 1], F32)
    E["alog"] = din("alog", [H, 1], F32)
    E["gnw"] = din("gnw", [128, 1], F32)
    E["qw_gu"] = din("L0_ffn_qweight_gu", [512, GU], U32)
    E["qz_gu"] = din("L0_ffn_qzeros_gu", [32, GU // 8], U32)
    E["sc_gu"] = din("L0_ffn_scales_gu", [32, GU], F32)
    E["qw_d"] = din("L0_ffn_qweight_down", [F // 8, D], U32)
    E["qz_d"] = din("L0_ffn_qzeros_down", [NF, D // 8], U32)
    E["sc_d"] = din("L0_ffn_scales_down", [NF, D], F32)
    E["ident16"] = din("c_ident16", [128, 128], BF16)
    E["identf"] = din("c_identf", [128, 128], F32)
    E["cosT"] = din("c_cosT", [128, T], BF16)
    E["sinT"] = din("c_sinT", [128, T], BF16)
    E["maskT"] = din("c_maskT", [128, 128], F32)
    E["shifts"] = din("c_shifts", [128, 1], U32)
    E["shift8"] = din("c_shift8", [128, 8], U32)
    E["ones1"] = din("c_ones1", [1, 128], BF16)
    E["ones1f"] = din("c_ones1f", [1, 128], F32)
    E["ones128"] = din("c_ones128", [128, 1], BF16)
    E["ones128f"] = din("c_ones128f", [128, 1], F32)
    E["selbank"] = din("c_selbank", [128, NK * NK], BF16)
    E["eb16"] = din("c_eb16", [16, 16 * 128], F32)

    out = nc.dram_tensor("out", [T, D], F32, kind="ExternalOutput").ap()
    dbg = {}
    for name, shape, dt in debug:
        dbg[name] = nc.dram_tensor(name, shape, dt, kind="ExternalOutput").ap()

    xn16 = nc.dram_tensor("xn16_h", [T, D], BF16, kind="Internal").ap()
    w16 = {}
    for nm, rows in [("wq", D), ("wk", KVH * HD), ("wv", KVH * HD),
                     ("wg", D), ("wo", D), ("win", H)]:
        w16[nm] = nc.dram_tensor(nm + "16_h", [rows, D], BF16, kind="Internal").ap()
    wqT_h = nc.dram_tensor("wqT_h", [H * D, 128], BF16, kind="Internal").ap()
    wgT_h = nc.dram_tensor("wgT_h", [H * D, 128], BF16, kind="Internal").ap()
    wkT_h = nc.dram_tensor("wkT_h", [KVH * D, 128], BF16, kind="Internal").ap()
    wvT_h = nc.dram_tensor("wvT_h", [KVH * D, 128], BF16, kind="Internal").ap()
    woT_h = nc.dram_tensor("woT_h", [D, D], BF16, kind="Internal").ap()
    wd16_h = nc.dram_tensor("wd16_h", [F, D], BF16, kind="Internal").ap()
    zd16_h = nc.dram_tensor("zd16_h", [NF, D], BF16, kind="Internal").ap()
    hT_h = nc.dram_tensor("hT_h", [F, T], BF16, kind="Internal").ap()
    hbar_h = nc.dram_tensor("hbar_h", [NF, T], BF16, kind="Internal").ap()
    ytn_h = nc.dram_tensor("ytn_h", [D, T], BF16, kind="Internal").ap()

    stack = contextlib.ExitStack()

    def salloc(name, shape, dt, side="left"):
        g = nc.sbuf_tensor("s_" + name, shape, dt, side=side)
        t = g.__enter__()
        return t, g

    def palloc(name, shape, dt):
        g = nc.psum_tensor("p_" + name, shape, dt)
        t = g.__enter__()
        return t, g

    def free(*guards):
        for g in guards:
            g.__exit__(None, None, None)

    GT = {}
    for nm, shape, dt in [
        ("ident16", [128, 128], BF16),
        ("shifts", [128, 1], U32), ("shift8", [128, 8], U32),
        ("ones1", [1, 128], BF16), ("ones1f", [1, 128], F32),
        ("ones128", [128, 1], BF16), ("ones128f", [128, 1], F32),
        ("gnw", [128, 1], F32), ("dtb", [H, 1], F32), ("alog", [H, 1], F32),
        ("sbarT", [NK, T], BF16),
    ]:
        GT[nm], _ = salloc(nm, shape, dt)
    RT = {}
    rg = {}
    for nm, shape, dt in [
        ("identf", [128, 128], F32),
        ("maskT", [128, 128], F32),
        ("eb16", [16, 16 * 128], F32),
        ("G_tm", [128, NT * H], F32), ("dt_tm", [128, NT * H], F32),
    ]:
        RT[nm], rg[nm] = salloc(nm, shape, dt, side="right")

    xnT, g_xnT = salloc("xnT", [128, NK * T], BF16)

    sem_names = ["tS", "vS", "aS", "poS",
                 "qc", "qck", "qcv", "qcw", "qcq", "qcg", "qco",
                 "qx0", "qx1", "qxs", "qxt",
                 "qzd", "qzdst", "qrep0", "qrep1", "qscd0", "qscd1", "qwst0", "qwst1",
                 "qsel", "qwin", "qwt", "qcs", "qrot", "qyt", "qtr0", "qtr1", "qst5",
                 "qyq", "qw2a", "qw2b", "qxr0", "qxr1", "qo0", "qo1",
                 "qr30", "qr31", "qsc30", "qsc31", "qz3", "qhst", "qhbr",
                 "qhT", "qwd0", "qwd1", "qwd2", "qzs4", "qhbl", "qor0", "qor1",
                 "qo4a", "qo4b", "qdbg"]
    sems = {s: stack.enter_context(nc.semaphore(s)) for s in sem_names}
    p = Prog(sems)

    stg5 = []
    g05 = []
    for i in range(2):
        t, g = salloc(f"stg5_{i}", [128, 512], BF16); g05.append(g); stg5.append(t)
    p1g = []
    xbuf, xnbuf = [], []
    t, g = salloc("xbuf0", [128, D], F32); p1g.append(g); xbuf.append(t); xbuf.append(t)
    t, g = salloc("xnbuf0", [128, D], BF16); p1g.append(g); xnbuf.append(t); xnbuf.append(t)
    sqdump, g = salloc("sqdump", [128, D], F32); p1g.append(g)
    cols = {}
    for nm in ["sum", "sq", "mean", "rstd", "t0"]:
        cols[nm], g = salloc("col_" + nm, [128, 2], F32); p1g.append(g)
    free(*reversed(p1g))

    p15a = []
    qzd_sb, g = salloc("qzd", [NF, D // 16], U32); p15a.append(g)
    zqu_sb, g = salloc("zqu", [NF, D // 2], U32); p15a.append(g)
    zqf_sb, g = salloc("zqf", [NF, D // 2], F32); p15a.append(g)
    scd_sb, g = salloc("scd", [NF, D // 2], F32); p15a.append(g)
    zd16_sb, g = salloc("zd16", [NF, D // 2], BF16); p15a.append(g)
    free(*reversed(p15a))
    p15b = []
    qdbuf = []
    for i in range(2):
        t, g = salloc(f"qdbuf{i}", [128, D // 2], U32); p15b.append(g); qdbuf.append(t)
    wdbuf = []
    for i in range(2):
        t, g = salloc(f"wdbuf{i}", [128, D], BF16); p15b.append(g); wdbuf.append(t)
    scr16d = []
    for i in range(2):
        t, g = salloc(f"scr16d{i}", [1, D // 2], BF16); p15b.append(g); scr16d.append(t)
    free(*reversed(p15b))

    prep = []
    selbank_sb, g = salloc("selbank", [128, NK * NK], BF16); prep.append(g)
    win16_sb, g = salloc("win16", [H, D], BF16); prep.append(g)
    winT16, g = salloc("winT16", [128, H * NK], BF16); prep.append(g)
    dtraw, g = salloc("dtraw", [H, T], F32); prep.append(g)
    spt1, g = salloc("spt1", [H, T], F32); prep.append(g)
    spt2, g = salloc("spt2", [H, T], F32); prep.append(g)
    acol, g = salloc("acol", [H, 1], F32); prep.append(g)
    free(*reversed(prep))

    free(*reversed(g05))
    p2g = []
    def a2(nm, shape, dt):
        t, g = salloc(nm, shape, dt)
        p2g.append(g)
        return t
    WT16 = a2("WT16", [128, D], BF16)
    kT16 = a2("kT16", [128, T], BF16)
    ktm16 = a2("ktm16", [128, T], BF16)
    vtm16 = a2("vtm16", [128, T], BF16)
    qT16 = a2("qT16", [128, T], BF16)
    y16 = a2("y16", [128, T], BF16)
    S_all = a2("S_all", [128, T], F32)
    sdec_all = a2("sdec_all", [128, T], BF16)
    r1 = a2("r1", [128, 512], F32)
    r2 = a2("r2", [128, 512], F32)
    cs_sl = a2("cs_sl", [128, 512], BF16)
    sn_sl = a2("sn_sl", [128, 512], BF16)
    gatechunk = a2("gatechunk", [128, 512], BF16)
    ynst = a2("ynst", [128, 512], BF16)
    grow_sb = a2("grow", [16, 128], F32)
    glrow = a2("glrow", [1, 16], F32)
    fmat = a2("fmat", [128, 16], F32)
    cdall = a2("cdall", [128, 16], F32)
    xc16 = a2("xc16", [128, 128], BF16)
    kdec16 = a2("kdec16", [128, 128], BF16)
    s16c = a2("s16c", [128, 128], BF16)
    expg16 = a2("expg16", [128, 128], BF16)
    dec_scr = a2("dec_scr", [128, 128], F32)
    sq_scr = a2("sq_scr", [128, 128], F32)
    ssum = a2("ssum", [1, T], F32)
    rinv = ssum
    free(*reversed(p2g))

    p2b = []
    ytq = []
    for i in range(NK):
        t, g = salloc(f"ytq{i}", [128, 512], BF16); p2b.append(g); ytq.append(t)
    WT2 = []
    for i in range(2):
        t, g = salloc(f"WT2_{i}", [128, 8 * 512], BF16); p2b.append(g); WT2.append(t)
    xrd, outw = [], []
    for i in range(2):
        t, g = salloc(f"xrd{i}", [128, 512], F32); p2b.append(g); xrd.append(t)
        t, g = salloc(f"outw{i}", [128, 512], F32); p2b.append(g); outw.append(t)
    free(*reversed(p2b))

    free(*[rg[nm] for nm in ["dt_tm", "G_tm", "eb16", "maskT", "identf"]])
    p3g = []
    def a3(nm, shape, dt):
        t, g = salloc(nm, shape, dt)
        p3g.append(g)
        return t
    W1g = a3("W1g", [128, 33 * 256], BF16)
    W1u = a3("W1u", [128, 33 * 256], BF16)
    qbuf = [a3("qbuf0", [128, 1024], U32), a3("qbuf1", [128, 1024], U32)]
    scrow3 = [a3("scrow0", [1, 1024], BF16), a3("scrow1", [1, 1024], BF16)]
    zpk = a3("zpk", [32, 32], U32)
    zun = a3("zun", [32, 256], U32)
    zf = a3("zf", [32, 256], F32)
    zs = a3("zs", [32, 256], F32)
    zb16 = {"g": a3("zb16g", [32, 256], BF16), "u": a3("zb16u", [32, 256], BF16)}
    gstash = [a3(f"gstash{i}", [128, T], BF16) for i in range(2)]
    h16sb = a3("h16sb", [128, T], BF16)
    hrow = a3("hrow", [1, T], BF16)
    free(*reversed(p3g))

    free(g_xnT)
    p4g = []
    def a4(nm, shape, dt):
        t, g = salloc(nm, shape, dt)
        p4g.append(g)
        return t
    hTh = a4("hTh", [128, NF * 1024], BF16)
    wdt = [a4(f"wdt{i}", [128, 512], BF16) for i in range(3)]
    zdsl = a4("zdsl", [NF, 512], BF16)
    hbarL = a4("hbarL", [NF, T], BF16)
    ord_ = [a4(f"ord{i}", [128, 512], F32) for i in range(2)]
    owr_ = [a4(f"owr{i}", [128, 512], F32) for i in range(2)]
    free(*reversed(p4g))

    psA, g_psA = palloc("psA", [128, 2048], F32)
    psB, g_psB = palloc("psB", [128, 2048], F32)
    free(g_psB, g_psA)
    ps_proj, q1 = palloc("ps_proj", [128, 1024], F32)
    ps_gb, q2 = palloc("ps_gb", [128, 128], F32)
    ps_sT, q3 = palloc("ps_sT", [128, 128], F32)
    ps_cs, q4 = palloc("ps_cs", [128, 128], F32)
    ps_y, q5 = palloc("ps_y", [128, 128], F32)
    ps_t16, q6 = palloc("ps_t16", [128, 128], BF16)
    ps_tf, q7 = palloc("ps_tf", [128, 128], F32)
    free(q7, q6, q5, q4, q3, q2, q1)
    psD, g_psD = palloc("psD", [128, 4096], F32)
    free(g_psD)

    CNT = {}

    def phase05(p):
        castq5 = {"wk": "qck", "wv": "qcv", "win": "qcw", "wq": "qcq", "wg": "qcg", "wo": "qco"}
        st_marks = {}
        gi = 0
        for wname, dstT, nob in [("wk", wkT_h, 2), ("wv", wvT_h, 2),
                                 ("wq", wqT_h, 8), ("wg", wgT_h, 8), ("wo", woT_h, 8)]:
            p.wait("sy", castq5[wname], CNT["cast_" + wname])
            for ob in range(nob):
                for kt in range(NK):
                    par = gi % 2
                    qtr = "qtr0" if par == 0 else "qtr1"
                    if gi >= 2:
                        p.wait("sy", "qst5", st_marks[gi - 2])
                    p.dma("sy", lambda e, wname=wname, ob=ob, kt=kt, par=par: e.dma_start(
                        out=stg5[par][:, :],
                        in_=w16[wname][512 * ob:512 * (ob + 1), 128 * kt:128 * (kt + 1)],
                        transpose=True), qtr)
                    trv = p.val(qtr)
                    p.wait("sc", qtr, trv)
                    if wname == "wo":
                        p.dma("sc", lambda e, ob=ob, kt=kt, par=par: e.dma_start(
                            out=woT_h[128 * kt:128 * (kt + 1), 512 * ob:512 * (ob + 1)],
                            in_=stg5[par][:, :]), "qst5")
                    else:
                        for j in range(4):
                            hb = 4 * ob + j
                            p.dma("sc", lambda e, dstT=dstT, hb=hb, kt=kt, j=j, par=par: e.dma_start(
                                out=dstT[D * hb + 128 * kt:D * hb + 128 * (kt + 1), :],
                                in_=stg5[par][:, 128 * j:128 * (j + 1)]), "qst5")
                    st_marks[gi] = p.val("qst5")
                    gi += 1
            CNT["pt_" + wname] = p.val("qst5")

    def phase1(p):
        ln_done, st_done = [], []
        for i in range(NT):
            b = i % 2
            if i >= 1:
                p.wait("sy", "vS", ln_done[i - 1])
            qx = "qx0" if b == 0 else "qx1"
            p.dma("sy", lambda e, i=i, b=b: e.dma_start(out=xbuf[b][:, :],
                                                        in_=E["x"][128 * i:128 * (i + 1), :]), qx)
            ldv = p.val(qx)
            p.wait("ve", qx, ldv)
            p.do("ve", lambda e, b=b: e.reduce_sum(out=cols["sum"][:, b:b + 1],
                                                   in_=xbuf[b][:, :], axis=mybir.AxisListType.X))
            p.wait("sc", qx, ldv)
            p.do("sc", lambda e, b=b: e.activation(out=sqdump[:, :], in_=xbuf[b][:, :],
                                                   func=AF.Square), inc="aS")
            p.wait("ve", "aS")
            p.do("ve", lambda e, b=b: e.reduce_sum(out=cols["sq"][:, b:b + 1],
                                                   in_=sqdump[:, :], axis=mybir.AxisListType.X))
            p.do("ve", lambda e, b=b: e.tensor_scalar_mul(out=cols["mean"][:, b:b + 1],
                                                          in0=cols["sum"][:, b:b + 1], scalar1=1.0 / D))
            p.do("ve", lambda e, b=b: e.tensor_scalar_mul(out=cols["sq"][:, b:b + 1],
                                                          in0=cols["sq"][:, b:b + 1], scalar1=1.0 / D))
            p.do("ve", lambda e, b=b: e.tensor_mul(out=cols["t0"][:, b:b + 1],
                                                   in0=cols["mean"][:, b:b + 1],
                                                   in1=cols["mean"][:, b:b + 1]))
            p.do("ve", lambda e, b=b: e.tensor_sub(out=cols["sq"][:, b:b + 1],
                                                   in0=cols["sq"][:, b:b + 1],
                                                   in1=cols["t0"][:, b:b + 1]))
            p.do("ve", lambda e, b=b: e.tensor_scalar_add(out=cols["sq"][:, b:b + 1],
                                                          in0=cols["sq"][:, b:b + 1], scalar1=EPS), inc="vS")
            p.wait("sc", "vS")
            p.do("sc", lambda e, b=b: e.activation(out=cols["rstd"][:, b:b + 1],
                                                   in_=cols["sq"][:, b:b + 1], func=AF.Ln), inc="aS")
            p.wait("ve", "aS")
            p.do("ve", lambda e, b=b: e.tensor_scalar_mul(out=cols["rstd"][:, b:b + 1],
                                                          in0=cols["rstd"][:, b:b + 1], scalar1=-0.5), inc="vS")
            p.wait("sc", "vS")
            p.do("sc", lambda e, b=b: e.activation(out=cols["rstd"][:, b:b + 1],
                                                   in_=cols["rstd"][:, b:b + 1], func=AF.Exp), inc="aS")
            p.wait("ve", "aS")
            if i >= 1:
                p.wait("ve", "qxs", st_done[i - 1])
            p.do("ve", lambda e, b=b: e.tensor_scalar(out=xnbuf[b][:, :], in0=xbuf[b][:, :],
                                                      scalar1=cols["mean"][:, b:b + 1],
                                                      scalar2=cols["rstd"][:, b:b + 1],
                                                      op0=OP.subtract, op1=OP.mult), inc="vS")
            ln_done.append(p.val("vS"))
            p.wait("po", "vS", ln_done[-1])
            p.dma("po", lambda e, i=i, b=b: e.dma_start(out=xn16[128 * i:128 * (i + 1), :],
                                                        in_=xnbuf[b][:, :]), "qxs")
            st_done.append(p.val("qxs"))
        p.wait("sy", "qxs", st_done[-1])
        for m in range(NK):
            p.dma("sy", lambda e, m=m: e.dma_start(out=xnT[:, T * m:T * (m + 1)],
                                                   in_=xn16[:, 128 * m:128 * (m + 1)], transpose=True), "qxt")
        CNT["xnT"] = p.val("qxt")

    def phase15(p):
        for oh in range(2):
            D2 = D // 2
            if oh == 1:
                p.wait("sy", "qzdst")
            p.dma("sy", lambda e, oh=oh: e.dma_start(
                out=qzd_sb[:, :], in_=E["qz_d"][:, (D // 16) * oh:(D // 16) * (oh + 1)]), "qzd")
            p.dma("sy", lambda e, oh=oh: e.dma_start(
                out=scd_sb[:, :], in_=E["sc_d"][:, D2 * oh:D2 * (oh + 1)]), "qzd")
            p.wait("ve", "qzd")
            CW = D // 16
            for u in range(8):
                p.dof("ve", lambda e, u=u, CW=CW: e.tensor_scalar(
                    out=zqu_sb[:, CW * u:CW * (u + 1)], in0=qzd_sb[:, :],
                    scalar1=GT["shift8"][0:NF, u:u + 1], scalar2=15.0,
                    op0=OP.logical_shift_right, op1=OP.bitwise_and))
            p.dof("ve", lambda e, CW=CW: e.tensor_copy(
                out=zqf_sb[:, :].rearrange("p (c u) -> p c u", u=8),
                in_=zqu_sb[:, :].rearrange("p (u c) -> p c u", u=8)))
            p.dof("ve", lambda e: e.tensor_scalar_add(out=zqf_sb[:, :], in0=zqf_sb[:, :], scalar1=1.0))
            p.dof("ve", lambda e: e.tensor_mul(out=zqf_sb[:, :], in0=zqf_sb[:, :], in1=scd_sb[:, :]))
            p.dof("ve", lambda e: e.tensor_scalar_mul(out=zd16_sb[:, :], in0=zqf_sb[:, :], scalar1=-1.0))
            p.wait("po", "vS")
            p.dma("po", lambda e, oh=oh: e.dma_start(
                out=zd16_h[:, D2 * oh:D2 * (oh + 1)], in_=zd16_sb[:, :]), "qzdst")

        p.barrier()
        wr_done = []
        tt_marks = {}
        bc_marks = {}
        D2 = D // 2
        for step in range(2 * NF):
            m, hh = step // 2, step % 2
            par = step % 2
            if step >= 2:
                p.wait("sy", "vS", tt_marks[step - 2])
            rep = E["qw_d"][16 * m:16 * (m + 1), D2 * hh:D2 * (hh + 1)].rearrange(
                "k (r j) -> k r j", r=1).broadcast_to([16, 8, D2])
            qrepq = "qrep0" if par == 0 else "qrep1"
            qscdq = "qscd0" if par == 0 else "qscd1"
            p.dma("sy", lambda e, rep=rep, par=par: e.dma_start(out=qdbuf[par][:, :], in_=rep), qrepq)
            qldv = p.val(qrepq)
            if step >= 2:
                p.wait("po", "tS", bc_marks[step - 2])
            p.dma("po", lambda e, m=m, hh=hh, par=par: e.dma_start(
                out=scr16d[par][:, :], in_=E["sc_d"][m:m + 1, D2 * hh:D2 * (hh + 1)]), qscdq)
            scldv = p.val(qscdq)
            p.wait("te", qscdq, scldv)
            if step >= 2:
                p.wait("te", "vS", tt_marks[step - 2])
            hs = psA if hh == 0 else psB
            for oc in range(4):
                p.do("te", lambda e, oc=oc, hs=hs, par=par: e.matmul(
                    hs[:, 512 * oc:512 * (oc + 1)],
                    GT["ones1"][:, :], scr16d[par][0:1, 512 * oc:512 * (oc + 1)],
                    start=True, stop=True), inc=("tS" if oc == 3 else None))
            bc_marks[step] = p.val("tS")
            p.wait("ve", qrepq, qldv)
            p.do("ve", lambda e, par=par: e.tensor_scalar(out=qdbuf[par][:, :], in0=qdbuf[par][:, :],
                                                          scalar1=GT["shifts"][:, 0:1], scalar2=15.0,
                                                          op0=OP.logical_shift_right, op1=OP.bitwise_and))
            p.wait("ve", "tS", bc_marks[step])
            if m >= 2 and hh == 0:
                p.wait("ve", "qwst0" if m % 2 == 0 else "qwst1", wr_done[m - 2])
            p.do("ve", lambda e, m=m, hh=hh, par=par, hs=hs: e.tensor_tensor(
                out=wdbuf[m % 2][:, D2 * hh:D2 * (hh + 1)], in0=qdbuf[par][:, :],
                in1=hs[:, :], op=OP.mult), inc="vS")
            tt_marks[step] = p.val("vS")
            if hh == 1:
                p.wait("sy", "vS", tt_marks[step])
                qw = "qwst0" if m % 2 == 0 else "qwst1"
                p.dma("sy", lambda e, m=m: e.dma_start(out=wd16_h[128 * m:128 * (m + 1), :],
                                                       in_=wdbuf[m % 2][:, :]), qw)
                wr_done.append(p.val(qw))
        CNT["wd_done0"] = p.val("qwst0")
        CNT["wd_done1"] = p.val("qwst1")

    def phase2prep(p):
        p.dma("sy", lambda e: e.dma_start(out=selbank_sb[:, :], in_=E["selbank"]), "qsel")
        p.wait("te", "qsel")
        p.wait("te", "qxt", CNT["xnT"])
        p.wait("te", "qc", CNT["consts"])
        p.wait("te", "vS")
        for nch in range(4):
            for m in range(NK):
                p.do("te", lambda e, nch=nch, m=m: e.matmul(
                    psA[0:NK, 512 * nch:512 * (nch + 1)],
                    selbank_sb[:, NK * m:NK * (m + 1)],
                    xnT[:, T * m + 512 * nch: T * m + 512 * (nch + 1)],
                    start=(m == 0), stop=(m == NK - 1)), inc=("tS" if m == NK - 1 else None))
            p.wait("ve", "tS")
            p.dof("ve", lambda e, nch=nch: e.tensor_copy(
                out=GT["sbarT"][:, 512 * nch:512 * (nch + 1)],
                in_=psA[0:NK, 512 * nch:512 * (nch + 1)]))
        CNT["sbar"] = p.val("vS")
        if "d_sbar" in dbg:
            p.wait("po", "vS", CNT["sbar"])
            p.dma("po", lambda e: e.dma_start(out=dbg["d_sbar"], in_=GT["sbarT"][:, :]), "qdbg")

        p.wait("sy", "qcw", CNT["cast_win"])
        p.dma("sy", lambda e: e.dma_start(out=win16_sb[:, :], in_=w16["win"]), "qwin")
        p.wait("te", "qwin")
        p.wait("te", "aS")
        psB16 = psB.bitcast(BF16)
        for m in range(NK):
            if m >= 2:
                p.wait("te", "aS")
            p.do("te", lambda e, m=m: e.transpose(
                psB16[:, 1024 * (m % 2):1024 * (m % 2) + H],
                win16_sb[0:H, 128 * m:128 * (m + 1)], GT["ident16"][0:H, 0:H]), inc="tS")
            p.wait("sc", "tS")
            p.do("sc", lambda e, m=m: e.activation(
                out=winT16[:, H * m:H * (m + 1)],
                in_=psB16[:, 1024 * (m % 2):1024 * (m % 2) + H], func=AF.Identity), inc="aS")
        p.wait("te", "aS")

        for nch in range(4):
            for m in range(NK):
                p.do("te", lambda e, nch=nch, m=m: e.matmul(
                    psA[0:H, 512 * nch:512 * (nch + 1)],
                    winT16[:, H * m:H * (m + 1)],
                    xnT[:, T * m + 512 * nch:T * m + 512 * (nch + 1)],
                    start=(m == 0), stop=(m == NK - 1)), inc=("tS" if m == NK - 1 else None))
        p.wait("sc", "tS")
        p.dof("sc", lambda e: e.activation(out=dtraw[:, :], in_=psA[0:H, 0:T],
                                           func=AF.Identity, bias=GT["dtb"][0:H, 0:1]))
        p.dof("sc", lambda e: e.activation(out=spt1[:, :], in_=dtraw[:, :], func=AF.Abs))
        p.dof("sc", lambda e: e.activation(out=spt1[:, :], in_=spt1[:, :], func=AF.Exp, scale=-1.0))
        p.dof("sc", lambda e: e.activation(out=spt1[:, :], in_=spt1[:, :], func=AF.Ln, bias=1.0))
        p.wait("ve", "aS")
        p.dof("ve", lambda e: e.tensor_scalar_max(out=spt2[:, :], in0=dtraw[:, :], scalar1=0.0))
        p.dof("ve", lambda e: e.tensor_add(out=dtraw[:, :], in0=spt2[:, :], in1=spt1[:, :]))
        p.wait("sc", "vS")
        p.dof("sc", lambda e: e.activation(out=acol[:, :], in_=GT["alog"][:, :], func=AF.Exp))
        p.wait("ve", "aS")
        p.dof("ve", lambda e: e.tensor_scalar_mul(out=acol[:, :], in0=acol[:, :], scalar1=-1.0))
        p.dof("ve", lambda e: e.tensor_scalar_mul(out=spt1[:, :], in0=dtraw[:, :],
                                                  scalar1=acol[0:H, 0:1]))
        for n in range(NT):
            p.dof("ve", lambda e, n=n: e.tensor_tensor_scan(
                out=spt2[:, 128 * n:128 * (n + 1)], data0=spt1[:, 128 * n:128 * (n + 1)],
                data1=spt1[:, 128 * n:128 * (n + 1)], initial=0.0,
                op0=OP.add, op1=OP.min))
        p.wait("te", "vS")
        for n in range(NT):
            p.do("te", lambda e, n=n: e.transpose(
                psB[:, 512 * (n % 2):512 * (n % 2) + H],
                dtraw[0:H, 128 * n:128 * (n + 1)], RT["identf"][0:H, 0:H]), inc="tS")
            p.wait("ve", "tS")
            p.do("ve", lambda e, n=n: e.tensor_copy(
                out=RT["dt_tm"][:, H * n:H * (n + 1)],
                in_=psB[:, 512 * (n % 2):512 * (n % 2) + H]), inc="vS")
            p.wait("te", "vS")
            p.do("te", lambda e, n=n: e.transpose(
                psB[:, 512 * (n % 2):512 * (n % 2) + H],
                spt2[0:H, 128 * n:128 * (n + 1)], RT["identf"][0:H, 0:H]), inc="tS")
            p.wait("ve", "tS")
            p.do("ve", lambda e, n=n: e.tensor_copy(
                out=RT["G_tm"][:, H * n:H * (n + 1)],
                in_=psB[:, 512 * (n % 2):512 * (n % 2) + H]), inc="vS")
        CNT["prep"] = p.val("vS")
        if "d_gtm" in dbg:
            p.wait("po", "vS", CNT["prep"])
            p.dma("po", lambda e: e.dma_start(out=dbg["d_gtm"], in_=RT["G_tm"][:, :]), "qdbg")
            p.dma("po", lambda e: e.dma_start(out=dbg["d_dttm"], in_=RT["dt_tm"][:, :]), "qdbg")

    def proj_rope(p, wsrc, orow, dest):
        for nch in range(4):
            bk = nch % 2
            p.wait("te", "vS")
            for m in range(NK):
                p.do("te", lambda e, nch=nch, bk=bk, m=m: e.matmul(
                    ps_proj[:, 512 * bk:512 * (bk + 1)],
                    WT16[:, 128 * m:128 * (m + 1)],
                    xnT[:, T * m + 512 * nch:T * m + 512 * (nch + 1)],
                    start=(m == 0), stop=(m == NK - 1)), inc=("tS" if m == NK - 1 else None))
            h0 = 512 * nch
            p.wait("sy", "vS")
            p.dma("sy", lambda e, h0=h0: e.dma_start(out=cs_sl[:, :], in_=E["cosT"][:, h0:h0 + 512]), "qcs")
            p.dma("sy", lambda e, h0=h0: e.dma_start(out=sn_sl[:, :], in_=E["sinT"][:, h0:h0 + 512]), "qcs")
            csl = p.val("qcs")
            p.wait("ve", "tS")
            if nch >= 1:
                p.wait("ve", "qrot")
            p.do("ve", lambda e, bk=bk: e.tensor_copy(out=r1[:, :],
                                                      in_=ps_proj[:, 512 * bk:512 * (bk + 1)]), inc="vS")
            rv = p.val("vS")
            p.wait("po", "vS", rv)
            p.dma("po", lambda e: e.dma_start(out=r2[0:64, :], in_=r1[64:128, :]), "qrot")
            p.dma("po", lambda e: e.dma_start(out=r2[64:128, :], in_=r1[0:64, :]), "qrot")
            rotv = p.val("qrot")
            p.wait("ve", "qrot", rotv)
            p.wait("ve", "qcs", csl)
            p.do("ve", lambda e: e.tensor_mul(out=r1[:, :], in0=r1[:, :], in1=cs_sl[:, :]))
            p.do("ve", lambda e: e.tensor_mul(out=r2[:, :], in0=r2[:, :], in1=sn_sl[:, :]))
            p.do("ve", lambda e, h0=h0, dest=dest: e.tensor_add(
                out=dest[:, h0:h0 + 512], in0=r1[:, :], in1=r2[:, :]), inc="vS")

    def load_wslice(p, whT, hb):
        inap = whT[D * hb:D * (hb + 1), :].rearrange("(kt i) o -> i kt o", i=128)
        outap = WT16[:, :].rearrange("p (kt o) -> p kt o", kt=NK)
        p.dma("sy", lambda e, inap=inap, outap=outap: e.dma_start(out=outap, in_=inap), "qwt")
        return p.val("qwt")

    def phase2(p):
        p.wait("po", "vS")
        ytn_marks = []
        for kv in range(KVH):
            p.wait("sy", "qst5", CNT["pt_wk"])
            p.wait("sy", "tS")
            wk_ld = load_wslice(p, wkT_h, kv)
            p.wait("te", "qwt", wk_ld)
            proj_rope(p, w16["wk"], 128 * kv, kT16)
            kv_rope = p.val("vS")
            p.wait("sy", "qst5", CNT["pt_wv"])
            p.wait("sy", "vS", kv_rope)
            p.wait("sy", "tS")
            wv_ld = load_wslice(p, wvT_h, kv)
            p.wait("te", "qwt", wv_ld)
            for tt in range(NT):
                p.wait("te", "vS")
                for m in range(NK):
                    p.do("te", lambda e, tt=tt, m=m, kv=kv: e.matmul(
                        ps_proj[:, 0:128],
                        xnT[:, T * m + 128 * tt:T * m + 128 * (tt + 1)],
                        WT16[:, 128 * m:128 * (m + 1)],
                        start=(m == 0), stop=(m == NK - 1)), inc=("tS" if m == NK - 1 else None))
                p.wait("ve", "tS")
                p.do("ve", lambda e, tt=tt: e.tensor_copy(out=vtm16[:, 128 * tt:128 * (tt + 1)],
                                                          in_=ps_proj[:, 0:128]), inc="vS")
            for n in range(NT):
                p.wait("te", "vS")
                p.do("te", lambda e, n=n: e.transpose(
                    ps_t16[:, :], kT16[:, 128 * n:128 * (n + 1)], GT["ident16"][:, :]), inc="tS")
                p.wait("ve", "tS")
                p.do("ve", lambda e, n=n: e.tensor_copy(out=ktm16[:, 128 * n:128 * (n + 1)],
                                                        in_=ps_t16[:, :]), inc="vS")

            for hj in range(4):
                h = 4 * kv + hj
                p.wait("sy", "qst5", CNT["pt_wq"])
                p.wait("sy", "tS")
                wq_ld = load_wslice(p, wqT_h, h)
                p.wait("te", "qwt", wq_ld)
                proj_rope(p, w16["wq"], 128 * h, qT16)

                gview = RT["G_tm"].rearrange("p (n hh) -> p n hh", hh=H)[:, :, h].rearrange("p n -> p n")
                p.wait("te", "vS")
                p.do("te", lambda e, gview=gview: e.transpose(
                    ps_tf[0:16, 0:128], gview, RT["identf"][:, :]), inc="tS")
                p.wait("sc", "tS")
                p.do("sc", lambda e: e.activation(out=grow_sb[:, :], in_=ps_tf[0:16, 0:128],
                                                  func=AF.Identity), inc="aS")
                p.wait("te", "aS")
                p.do("te", lambda e: e.transpose(ps_tf[0:1, 0:16], grow_sb[:, 127:128],
                                                 RT["identf"][0:16, 0:16]), inc="tS")
                p.wait("sc", "tS")
                p.do("sc", lambda e: e.activation(out=glrow[:, :], in_=ps_tf[0:1, 0:16],
                                                  func=AF.Identity), inc="aS")
                p.wait("te", "aS")
                p.do("te", lambda e: e.matmul(ps_tf[:, 0:16], GT["ones1f"][:, :], glrow[0:1, :],
                                              start=True, stop=True), inc="tS")
                p.wait("sc", "tS")
                p.do("sc", lambda e: e.activation(out=cdall[:, :], in_=ps_tf[:, 0:16],
                                                  func=AF.Exp), inc="aS")
                p.wait("ve", "tS")
                p.do("ve", lambda e, gview=gview: e.tensor_sub(out=fmat[:, :], in0=ps_tf[:, 0:16],
                                                               in1=gview), inc="vS")
                p.wait("sc", "vS")
                p.do("sc", lambda e: e.activation(out=fmat[:, :], in_=fmat[:, :], func=AF.Exp), inc="aS")
                fcd = p.val("aS")

                for n in range(NT):
                    dtcol = RT["dt_tm"][:, H * n + h:H * n + h + 1]
                    gcol = RT["G_tm"][:, H * n + h:H * n + h + 1]
                    p.wait("te", "vS")
                    p.do("te", lambda e, n=n: e.matmul(
                        ps_gb[:, :], RT["eb16"][:, 128 * n:128 * (n + 1)], grow_sb[:, :],
                        start=True, stop=True), inc="tS")
                    p.wait("ve", "tS")
                    p.do("ve", lambda e, gcol=gcol: e.scalar_tensor_tensor(
                        out=dec_scr[:, :], in0=ps_gb[:, :], scalar=gcol, in1=RT["maskT"][:, :],
                        op0=OP.subtract, op1=OP.add), inc="vS")
                    p.wait("sc", "vS")
                    p.wait("sc", "aS", fcd)
                    p.do("sc", lambda e: e.activation(out=expg16[:, :], in_=dec_scr[:, :],
                                                      func=AF.Exp), inc="aS")
                    p.wait("te", "vS")
                    p.do("te", lambda e, n=n: e.matmul(
                        ps_sT[:, :], kT16[:, 128 * n:128 * (n + 1)], qT16[:, 128 * n:128 * (n + 1)],
                        start=True, stop=True), inc="tS")
                    p.wait("ve", "tS")
                    p.wait("ve", "aS")
                    p.do("ve", lambda e, n=n: e.tensor_tensor(
                        out=sdec_all[:, 128 * n:128 * (n + 1)], in0=ps_sT[:, :], in1=expg16[:, :],
                        op=OP.mult))
                    p.do("ve", lambda e, n=n, dtcol=dtcol: e.tensor_scalar_mul(
                        out=xc16[:, :], in0=vtm16[:, 128 * n:128 * (n + 1)], scalar1=dtcol))
                    p.do("ve", lambda e, n=n: e.tensor_scalar_mul(
                        out=kdec16[:, :], in0=ktm16[:, 128 * n:128 * (n + 1)],
                        scalar1=fmat[:, n:n + 1]), inc="vS")
                    p.wait("te", "vS")
                    p.do("te", lambda e: e.matmul(ps_cs[:, :], kdec16[:, :], xc16[:, :],
                                                  start=True, stop=True), inc="tS")
                    p.wait("ve", "tS")
                    p.do("ve", lambda e, n=n: e.tensor_copy(out=S_all[:, 128 * n:128 * (n + 1)],
                                                            in_=ps_cs[:, :]), inc="vS")

                for n in range(1, NT):
                    s_prev = S_all[:, 128 * (n - 1):128 * n]
                    s_cur = S_all[:, 128 * n:128 * (n + 1)]
                    p.do("ve", lambda e, s_prev=s_prev, s_cur=s_cur, n=n: e.scalar_tensor_tensor(
                        out=s_cur, in0=s_prev, scalar=cdall[:, n:n + 1], in1=s_cur,
                        op0=OP.mult, op1=OP.add), inc="vS")

                for n in range(NT):
                    dtcol = RT["dt_tm"][:, H * n + h:H * n + h + 1]
                    p.wait("te", "vS")
                    p.wait("te", "aS")
                    p.do("te", lambda e, n=n: e.matmul(
                        ps_gb[:, :], RT["eb16"][:, 128 * n:128 * (n + 1)], grow_sb[:, :],
                        start=True, stop=True), inc="tS")
                    p.wait("sc", "tS")
                    p.do("sc", lambda e: e.activation(out=expg16[:, :], in_=ps_gb[:, :],
                                                      func=AF.Exp), inc="aS")
                    p.wait("ve", "aS")
                    p.do("ve", lambda e, n=n: e.tensor_tensor(
                        out=expg16[:, :], in0=qT16[:, 128 * n:128 * (n + 1)],
                        in1=expg16[:, :], op=OP.mult))
                    p.do("ve", lambda e, n=n, dtcol=dtcol: e.tensor_scalar_mul(
                        out=xc16[:, :], in0=vtm16[:, 128 * n:128 * (n + 1)], scalar1=dtcol), inc="vS")
                    if n >= 1:
                        s_prev = S_all[:, 128 * (n - 1):128 * n]
                        p.do("ve", lambda e, s_prev=s_prev: e.tensor_copy(out=s16c[:, :], in_=s_prev), inc="vS")
                    p.wait("te", "vS")
                    if n >= 1:
                        p.do("te", lambda e: e.matmul(ps_y[:, :], s16c[:, :], expg16[:, :],
                                                      start=True, stop=False))
                    p.do("te", lambda e, n=n: e.matmul(
                        ps_y[:, :], xc16[:, :], sdec_all[:, 128 * n:128 * (n + 1)],
                        start=(n == 0), stop=True), inc="tS")
                    p.wait("ve", "tS")
                    p.do("ve", lambda e, n=n: e.tensor_copy(out=y16[:, 128 * n:128 * (n + 1)],
                                                            in_=ps_y[:, :]), inc="vS")
                    p.wait("sc", "vS")
                    p.do("sc", lambda e: e.activation(out=sq_scr[:, :], in_=ps_y[:, :],
                                                      func=AF.Square), inc="aS")
                    p.wait("te", "aS")
                    p.do("te", lambda e: e.matmul(ps_tf[0:1, 0:128], GT["ones128f"][:, :],
                                                  sq_scr[:, :], start=True, stop=True), inc="tS")
                    p.wait("ve", "tS")
                    p.do("ve", lambda e, n=n: e.tensor_copy(
                        out=ssum[0:1, 128 * n:128 * (n + 1)], in_=ps_tf[0:1, 0:128]), inc="vS")

                p.do("ve", lambda e: e.tensor_scalar(out=rinv[:, :], in0=ssum[:, :],
                                                     scalar1=1.0 / 128.0, scalar2=EPS,
                                                     op0=OP.mult, op1=OP.add), inc="vS")
                p.wait("sc", "vS")
                p.do("sc", lambda e: e.activation(out=rinv[:, :], in_=rinv[:, :], func=AF.Ln), inc="aS")
                p.wait("ve", "aS")
                p.do("ve", lambda e: e.tensor_scalar_mul(out=rinv[:, :], in0=rinv[:, :],
                                                         scalar1=-0.5), inc="vS")
                p.wait("sc", "vS")
                p.do("sc", lambda e: e.activation(out=rinv[:, :], in_=rinv[:, :], func=AF.Exp), inc="aS")
                rinv_done = p.val("aS")

                p.wait("sy", "qst5", CNT["pt_wg"])
                p.wait("sy", "tS")
                wg_ld = load_wslice(p, wgT_h, h)
                p.wait("te", "qwt", wg_ld)
                for nch in range(4):
                    p.wait("te", "vS")
                    for m in range(NK):
                        p.do("te", lambda e, nch=nch, m=m: e.matmul(
                            ps_proj[:, 512 * (nch % 2):512 * (nch % 2 + 1)],
                            WT16[:, 128 * m:128 * (m + 1)],
                            xnT[:, T * m + 512 * nch:T * m + 512 * (nch + 1)],
                            start=(m == 0), stop=(m == NK - 1)), inc=("tS" if m == NK - 1 else None))
                    p.wait("sc", "tS")
                    p.wait("sc", "aS", rinv_done)
                    p.do("sc", lambda e, nch=nch: e.activation(
                        out=gatechunk[:, :], in_=ps_proj[:, 512 * (nch % 2):512 * (nch % 2 + 1)],
                        func=AF.Silu), inc="aS")
                    for s4 in range(4):
                        tpos = 512 * nch + 128 * s4
                        p.wait("te", "aS")
                        p.wait("te", "vS")
                        p.do("te", lambda e, tpos=tpos: e.matmul(
                            ps_gb[:, :], GT["ones1f"][:, :], rinv[0:1, tpos:tpos + 128],
                            start=True, stop=True), inc="tS")
                        p.wait("ve", "tS")
                        p.wait("ve", "aS")
                        p.do("ve", lambda e, tpos=tpos: e.tensor_tensor(
                            out=dec_scr[:, :], in0=ps_gb[:, :], in1=y16[:, tpos:tpos + 128],
                            op=OP.mult))
                        p.do("ve", lambda e, s4=s4: e.scalar_tensor_tensor(
                            out=ynst[:, 128 * s4:128 * (s4 + 1)], in0=dec_scr[:, :],
                            scalar=GT["gnw"][:, 0:1], in1=gatechunk[:, 128 * s4:128 * (s4 + 1)],
                            op0=OP.mult, op1=OP.mult), inc="vS")
                    ynv = p.val("vS")
                    p.wait("po", "vS", ynv)
                    p.dma("po", lambda e, h=h, nch=nch: e.dma_start(
                        out=ytn_h[128 * h:128 * (h + 1), 512 * nch:512 * (nch + 1)],
                        in_=ynst[:, :]), "qyt")
                    p.wait("ve", "qyt")
        CNT["ytn"] = p.val("qyt")

    def phase2b(p):
        p.wait("sy", "qyt", CNT["ytn"])
        st_marks = []
        for tq in range(4):
            p.wait("sy", "tS")
            for m in range(NK):
                p.dma("sy", lambda e, m=m, tq=tq: e.dma_start(
                    out=ytq[m][:, :], in_=ytn_h[128 * m:128 * (m + 1), 512 * tq:512 * (tq + 1)]), "qyq")
            yld = p.val("qyq")
            p.wait("te", "qyq", yld)
            for oc in range(8):
                p.wait("sy", "qst5", CNT["pt_wo"])
                kgmarks = []
                for kg in range(4):
                    if kg >= 2:
                        p.wait("sy", "tS", kgmarks[kg - 2])
                    qw2 = "qw2a" if kg % 2 == 0 else "qw2b"
                    inap = woT_h[1024 * kg:1024 * (kg + 1), 512 * oc:512 * (oc + 1)].rearrange(
                        "(k i) o -> i k o", i=128)
                    outap = WT2[kg % 2][:, :].rearrange("p (k o) -> p k o", k=8)
                    p.dma("sy", lambda e, inap=inap, outap=outap: e.dma_start(
                        out=outap, in_=inap), qw2)
                    wld = p.val(qw2)
                    p.wait("te", qw2, wld)
                    if kg == 0 and (oc > 0 or tq > 0):
                        p.wait("te", "vS")
                    for k in range(8):
                        kt = 8 * kg + k
                        for tt in range(4):
                            p.do("te", lambda e, kt=kt, k=k, kg=kg, tt=tt: e.matmul(
                                psD[:, 512 * tt:512 * (tt + 1)],
                                ytq[kt][:, 128 * tt:128 * (tt + 1)],
                                WT2[kg % 2][:, 512 * k:512 * (k + 1)],
                                start=(kg == 0 and k == 0), stop=(kg == 3 and k == 7)),
                                inc=("tS" if (k == 7 and tt == 3) else None))
                    kgmarks.append(p.val("tS"))
                mmv = p.val("tS")
                for tt in range(4):
                    trow = 512 * tq + 128 * tt
                    qxr = "qxr0" if tt % 2 == 0 else "qxr1"
                    qo = "qo0" if tt % 2 == 0 else "qo1"
                    if len(st_marks) >= 2:
                        p.wait("sy", "qo0" if st_marks[-2][0] == 0 else "qo1", st_marks[-2][1])
                    p.dma("sy", lambda e, trow=trow, oc=oc, tt=tt: e.dma_start(
                        out=xrd[tt % 2][:, :],
                        in_=E["x"][trow:trow + 128, 512 * oc:512 * (oc + 1)]), qxr)
                    xldv = p.val(qxr)
                    p.wait("ve", "tS", mmv)
                    p.wait("ve", qxr, xldv)
                    p.wait("ve", qo)
                    p.do("ve", lambda e, tt=tt: e.tensor_add(
                        out=outw[tt % 2][:, :],
                        in0=psD[:, 512 * tt:512 * (tt + 1)], in1=xrd[tt % 2][:, :]), inc="vS")
                    owv = p.val("vS")
                    p.wait("po", "vS", owv)
                    p.dma("po", lambda e, trow=trow, oc=oc, tt=tt: e.dma_start(
                        out=out[trow:trow + 128, 512 * oc:512 * (oc + 1)],
                        in_=outw[tt % 2][:, :]), qo)
                    st_marks.append((tt % 2, p.val(qo)))
        CNT["out_attn0"] = p.val("qo0")
        CNT["out_attn1"] = p.val("qo1")

    def phase3(p):
        st3 = {"tt": {}, "ttg": [], "bcg": []}

        def dequant_side(p, c, side):
            jbase = 256 * c + (0 if side == "g" else F)
            W1buf = W1g if side == "g" else W1u
            key = (c, side)
            p.wait("sy", "vS")
            p.dma("sy", lambda e, jbase=jbase: e.dma_start(
                out=zpk[:, :], in_=E["qz_gu"][:, jbase // 8:jbase // 8 + 32]), "qz3")
            p.dma("sy", lambda e, jbase=jbase: e.dma_start(
                out=zs[:, :], in_=E["sc_gu"][0:32, jbase:jbase + 256]), "qz3")
            zld = p.val("qz3")
            p.wait("ve", "qz3", zld)
            p.wait("ve", "tS")
            for u in range(8):
                p.dof("ve", lambda e, u=u: e.tensor_scalar(
                    out=zun[:, 32 * u:32 * (u + 1)], in0=zpk[:, :],
                    scalar1=GT["shift8"][0:32, u:u + 1], scalar2=15.0,
                    op0=OP.logical_shift_right, op1=OP.bitwise_and))
            p.dof("ve", lambda e: e.tensor_copy(
                out=zf[:, :].rearrange("p (cc u) -> p cc u", u=8),
                in_=zun[:, :].rearrange("p (u cc) -> p cc u", u=8)))
            p.dof("ve", lambda e: e.tensor_scalar_add(out=zf[:, :], in0=zf[:, :], scalar1=1.0))
            p.dof("ve", lambda e: e.tensor_mul(out=zf[:, :], in0=zf[:, :], in1=zs[:, :]))
            p.dof("ve", lambda e, side=side: e.tensor_scalar_mul(
                out=zb16[side][:, :], in0=zf[:, :], scalar1=-1.0))
            st3["tt"][(key, "z")] = p.val("vS")

            for h8 in range(8):
                gi = len(st3["ttg"])
                par = h8 % 2
                if gi >= 2:
                    p.wait("sy", "vS", st3["ttg"][gi - 2])
                qr3q = "qr30" if par == 0 else "qr31"
                qsc3q = "qsc30" if par == 0 else "qsc31"
                for ktl in range(4):
                    kt = 4 * h8 + ktl
                    r0 = 16 * kt
                    rep = E["qw_gu"][r0:r0 + 16, jbase:jbase + 256].rearrange(
                        "k (r j) -> k r j", r=1).broadcast_to([16, 8, 256])
                    p.dma("sy", lambda e, rep=rep, ktl=ktl, par=par: e.dma_start(
                        out=qbuf[par][:, 256 * ktl:256 * (ktl + 1)], in_=rep), qr3q)
                qldv = p.val(qr3q)
                screin = E["sc_gu"][4 * h8:4 * h8 + 4, jbase:jbase + 256]
                if gi >= 2:
                    p.wait("po", "tS", st3["bcg"][gi - 2])
                p.dma("po", lambda e, screin=screin, par=par: e.dma_start(
                    out=scrow3[par][:, :].rearrange("o (kt j) -> o kt j", kt=4),
                    in_=screin.rearrange("(o kt) j -> o kt j", o=1)), qsc3q)
                sc16v = p.val(qsc3q)
                p.wait("te", qsc3q, sc16v)
                if gi >= 2:
                    p.wait("te", "vS", st3["ttg"][gi - 2])
                else:
                    p.wait("te", "vS")
                for u in range(2):
                    p.do("te", lambda e, u=u, par=par: e.matmul(
                        psA[:, 1024 * par + 512 * u:1024 * par + 512 * (u + 1)],
                        GT["ones1"][:, :], scrow3[par][0:1, 512 * u:512 * (u + 1)],
                        start=True, stop=True), inc=("tS" if u == 1 else None))
                scv = p.val("tS")
                st3["bcg"].append(scv)
                p.wait("ve", qr3q, qldv)
                p.do("ve", lambda e, par=par: e.tensor_scalar(
                    out=qbuf[par][:, :], in0=qbuf[par][:, :],
                    scalar1=GT["shifts"][:, 0:1], scalar2=15.0,
                    op0=OP.logical_shift_right, op1=OP.bitwise_and))
                p.wait("ve", "tS", scv)
                p.do("ve", lambda e, W1buf=W1buf, h8=h8, par=par: e.tensor_tensor(
                    out=W1buf[:, 1024 * h8:1024 * (h8 + 1)], in0=qbuf[par][:, :],
                    in1=psA[:, 1024 * par:1024 * (par + 1)], op=OP.mult), inc="vS")
                st3["tt"][(key, h8)] = p.val("vS")
                st3["ttg"].append(p.val("vS"))

        def mm_side(p, c, side):
            W1buf = W1g if side == "g" else W1u
            key = (c, side)
            if "d_w21" in dbg and c == DBG_C and side == "g":
                p.wait("po", "vS", st3["tt"][(key, "z")])
                p.dma("po", lambda e: e.dma_start(out=dbg["d_w21"], in_=W1g[:, 0:8192]), "qdbg")
                p.dma("po", lambda e: e.dma_start(out=dbg["d_z21"], in_=zb16["g"][:, :]), "qdbg")
                p.dma("po", lambda e: e.dma_start(out=dbg["d_w21u"], in_=W1u[:, 0:8192]), "qdbg")
            p.wait("te", "vS", st3["tt"][(key, "z")])
            for mt in range(2):
                if side == "g":
                    p.wait("te", "aS")
                    p.wait("te", "vS")
                else:
                    p.wait("te", "aS")
                    p.wait("te", "vS")
                for nch in range(4):
                    for kt in range(33):
                        if kt < 32:
                            lhs = W1buf[:, 256 * kt + 128 * mt:256 * kt + 128 * (mt + 1)]
                            rhs = xnT[:, T * kt + 512 * nch:T * kt + 512 * (nch + 1)]
                        else:
                            lhs = zb16[side][:, 128 * mt:128 * (mt + 1)]
                            rhs = GT["sbarT"][:, 512 * nch:512 * (nch + 1)]
                        p.do("te", lambda e, lhs=lhs, rhs=rhs, nch=nch, kt=kt: e.matmul(
                            psB[:, 512 * nch:512 * (nch + 1)], lhs, rhs,
                            start=(kt == 0), stop=(kt == 32)),
                            inc=("tS" if (kt == 32 and nch == 3) else None))
                mmv = p.val("tS")
                if side == "g":
                    p.wait("sc", "tS", mmv)
                    for nch in range(4):
                        p.do("sc", lambda e, mt=mt, nch=nch: e.activation(
                            out=gstash[mt][:, 512 * nch:512 * (nch + 1)],
                            in_=psB[:, 512 * nch:512 * (nch + 1)], func=AF.Silu),
                            inc=("aS" if nch == 3 else None))
                else:
                    p.wait("ve", "tS", mmv)
                    p.wait("ve", "qhst")
                    for nch in range(4):
                        p.do("ve", lambda e, mt=mt, nch=nch: e.tensor_tensor(
                            out=h16sb[:, 512 * nch:512 * (nch + 1)],
                            in0=gstash[mt][:, 512 * nch:512 * (nch + 1)],
                            in1=psB[:, 512 * nch:512 * (nch + 1)], op=OP.mult),
                            inc=("vS" if nch == 3 else None))
                    hv = p.val("vS")
                    fidx = 2 * c + mt
                    p.wait("sy", "vS", hv)
                    p.dma("sy", lambda e, fidx=fidx: e.dma_start(
                        out=hT_h[128 * fidx:128 * (fidx + 1), :], in_=h16sb[:, :]), "qhst")
                    p.wait("te", "vS", hv)
                    for nch in range(4):
                        p.do("te", lambda e, nch=nch: e.matmul(
                            psB[0:1, 512 * nch:512 * (nch + 1)], GT["ones128"][:, :],
                            h16sb[:, 512 * nch:512 * (nch + 1)],
                            start=True, stop=True), inc=("tS" if nch == 3 else None))
                    p.wait("ve", "tS")
                    p.wait("ve", "qhbr")
                    p.dof("ve", lambda e: e.tensor_copy(out=hrow[:, :], in_=psB[0:1, 0:T]))
                    hrv = p.val("vS")
                    p.wait("sy", "vS", hrv)
                    p.dma("sy", lambda e, fidx=fidx: e.dma_start(
                        out=hbar_h[fidx:fidx + 1, :], in_=hrow[:, :]), "qhbr")

        dequant_side(p, 0, "g")
        dequant_side(p, 0, "u")
        for c in range(43):
            mm_side(p, c, "g")
            if c + 1 < 43:
                dequant_side(p, c + 1, "g")
            mm_side(p, c, "u")
            if c + 1 < 43:
                dequant_side(p, c + 1, "u")
        CNT["hT"] = p.val("qhst")
        CNT["hbar"] = p.val("qhbr")

    def phase4(p):
        p.wait("sy", "qwst0", CNT["wd_done0"])
        p.wait("sy", "qwst1", CNT["wd_done1"])
        p.wait("sy", "qhst", CNT["hT"])
        p.wait("sy", "qhbr", CNT["hbar"])
        p.dma("sy", lambda e: e.dma_start(out=hbarL[:, :], in_=hbar_h), "qhbl")
        st_marks = []
        for th in range(2):
            p.wait("sy", "tS")
            for kt in range(NF):
                p.dma("sy", lambda e, kt=kt, th=th: e.dma_start(
                    out=hTh[:, 1024 * kt:1024 * (kt + 1)],
                    in_=hT_h[128 * kt:128 * (kt + 1), 1024 * th:1024 * (th + 1)]), "qhT")
            hld = p.val("qhT")
            for oc in range(8):
                p.wait("sy", "tS")
                p.wait("sy", "qzdst", p.val("qzdst"))
                p.dma("sy", lambda e, oc=oc: e.dma_start(
                    out=zdsl[:, :], in_=zd16_h[:, 512 * oc:512 * (oc + 1)]), "qzs4")
                wd_marks = []
                mm_marks = []
                qwdq = ["qwd0", "qwd1", "qwd2"]
                for pre in range(2):
                    p.dma("sy", lambda e, oc=oc, pre=pre: e.dma_start(
                        out=wdt[pre][:, :], in_=wd16_h[128 * pre:128 * (pre + 1),
                                                       512 * oc:512 * (oc + 1)]), qwdq[pre])
                    wd_marks.append((qwdq[pre], p.val(qwdq[pre])))
                for kt in range(NF + 1):
                    if kt + 2 < NF:
                        if kt + 2 >= 3:
                            p.wait("sy", "tS", mm_marks[kt - 1] if kt - 1 < len(mm_marks) else p.val("tS"))
                        qwd = qwdq[(kt + 2) % 3]
                        p.dma("sy", lambda e, kt=kt, oc=oc: e.dma_start(
                            out=wdt[(kt + 2) % 3][:, :],
                            in_=wd16_h[128 * (kt + 2):128 * (kt + 3), 512 * oc:512 * (oc + 1)]), qwd)
                        wd_marks.append((qwd, p.val(qwd)))
                    if kt < NF:
                        p.wait("te", wd_marks[kt][0], wd_marks[kt][1])
                    else:
                        p.wait("te", "qzs4")
                        p.wait("te", "qhbl")
                        p.wait("te", "qhT", hld)
                    if kt == 0 and (oc > 0 or th > 0):
                        p.wait("te", "vS")
                    for tt in range(8):
                        if kt < NF:
                            lhs = hTh[:, 1024 * kt + 128 * tt:1024 * kt + 128 * (tt + 1)]
                            rhs = wdt[kt % 3][:, :]
                        else:
                            lhs = hbarL[:, 1024 * th + 128 * tt:1024 * th + 128 * (tt + 1)]
                            rhs = zdsl[:, :]
                        p.do("te", lambda e, lhs=lhs, rhs=rhs, tt=tt, kt=kt: e.matmul(
                            psD[:, 512 * tt:512 * (tt + 1)], lhs, rhs,
                            start=(kt == 0), stop=(kt == NF)),
                            inc=("tS" if tt == 7 else None))
                    mm_marks.append(p.val("tS"))
                mmv = p.val("tS")
                for tt in range(8):
                    trow = 1024 * th + 128 * tt
                    qor = "qor0" if tt % 2 == 0 else "qor1"
                    qo4 = "qo4a" if tt % 2 == 0 else "qo4b"
                    if len(st_marks) >= 2:
                        p.wait("sy", st_marks[-2][0], st_marks[-2][1])
                    rsrc = out if DO_ATTN else E["x"]
                    if DO_ATTN:
                        p.wait("sy", "qo0", CNT["out_attn0"])
                        p.wait("sy", "qo1", CNT["out_attn1"])
                    p.dma("sy", lambda e, trow=trow, oc=oc, tt=tt, rsrc=rsrc: e.dma_start(
                        out=ord_[tt % 2][:, :], in_=rsrc[trow:trow + 128, 512 * oc:512 * (oc + 1)]), qor)
                    xldv = p.val(qor)
                    p.wait("ve", "tS", mmv)
                    p.wait("ve", qor, xldv)
                    p.wait("ve", qo4)
                    p.do("ve", lambda e, tt=tt: e.tensor_add(
                        out=owr_[tt % 2][:, :], in0=psD[:, 512 * tt:512 * (tt + 1)],
                        in1=ord_[tt % 2][:, :]), inc="vS")
                    owv = p.val("vS")
                    p.wait("po", "vS", owv)
                    p.dma("po", lambda e, trow=trow, oc=oc, tt=tt: e.dma_start(
                        out=out[trow:trow + 128, 512 * oc:512 * (oc + 1)],
                        in_=owr_[tt % 2][:, :]), qo4)
                    st_marks.append((qo4, p.val(qo4)))

    def build(p):
        for nm, src, dst in [("ident16", E["ident16"], GT["ident16"]),
                             ("identf", E["identf"], RT["identf"]),
                             ("maskT", E["maskT"], RT["maskT"]),
                             ("eb16", E["eb16"], RT["eb16"]),
                             ("shifts", E["shifts"], GT["shifts"]),
                             ("shift8", E["shift8"], GT["shift8"]),
                             ("ones1", E["ones1"], GT["ones1"]),
                             ("ones1f", E["ones1f"], GT["ones1f"]),
                             ("ones128", E["ones128"], GT["ones128"]),
                             ("ones128f", E["ones128f"], GT["ones128f"]),
                             ("gnw", E["gnw"], GT["gnw"]),
                             ("dtb", E["dtb"], GT["dtb"]),
                             ("alog", E["alog"], GT["alog"])]:
            p.dma("sy", lambda e, dst=dst, src=src: e.dma_start(out=dst[:, :], in_=src), "qc")
        CNT["consts"] = p.val("qc")

        castq = {"wk": "qck", "wv": "qcv", "win": "qcw", "wq": "qcq", "wg": "qcg", "wo": "qco"}
        for wname in ["wk", "wv", "win", "wq", "wg", "wo"]:
            w32 = E[wname]
            wb = w16[wname]
            rows = w32.shape[0]
            nchunk = max(1, rows // 512)
            step = rows // nchunk
            for cidx in range(nchunk):
                p.dma("po", lambda e, a=w32, b=wb, r0=cidx * step, r1=(cidx + 1) * step:
                      e.dma_start(out=b[r0:r1, :], in_=a[r0:r1, :]), castq[wname])
            CNT["cast_" + wname] = p.val(castq[wname])

        phase1(p)
        phase05(p)
        p.barrier()
        if "d_xnT" in dbg:
            p.wait("po", "qxt", CNT["xnT"])
            p.dma("po", lambda e: e.dma_start(out=dbg["d_xnT"], in_=xnT[:, 0:NK * T]), "qdbg")
        if DO_FFN:
            phase15(p)
            p.barrier()
        if "d_xnT2" in dbg:
            p.wait("po", "vS")
            p.dma("po", lambda e: e.dma_start(out=dbg["d_xnT2"], in_=xnT[:, 0:NK * T]), "qdbg")
        phase2prep(p)
        p.barrier()
        if DO_ATTN:
            phase2(p)
            p.barrier()
            phase2b(p)
            p.barrier()
        if DO_FFN:
            phase3(p)
            p.barrier()
            phase4(p)
        p.barrier()
        if "d_hbar" in dbg:
            for mm in range(1):
                p.dma("po", lambda e: e.dma_start(out=dbg["d_hbar"], in_=hbar_h), "qdbg")
            p.dma("po", lambda e: e.dma_start(out=dbg["d_zd"], in_=zd16_h), "qdbg")
        if "d_ytn" in dbg:
            for mm in range(NK):
                p.dma("po", lambda e, mm=mm: e.dma_start(out=dbg["d_ytn"][128 * mm:128 * (mm + 1), :],
                                                         in_=ytn_h[128 * mm:128 * (mm + 1), :]), "qdbg")
        if "d_wd" in dbg:
            for mm in range(NF):
                p.dma("po", lambda e, mm=mm: e.dma_start(out=dbg["d_wd"][128 * mm:128 * (mm + 1), :],
                                                         in_=wd16_h[128 * mm:128 * (mm + 1), :]), "qdbg")
        if "d_ht" in dbg:
            for mm in range(NF):
                p.dma("po", lambda e, mm=mm: e.dma_start(out=dbg["d_ht"][128 * mm:128 * (mm + 1), :],
                                                         in_=hT_h[128 * mm:128 * (mm + 1), :]), "qdbg")
        for s in ["qo4a", "qo4b", "qo0", "qo1", "qdbg", "qxs"]:
            p.wait("po", s, p.val(s))

    with nc.Block() as block:
        @block.sync
        def _(eng):
            p.run("sy", eng, build)

        @block.tensor
        def _(eng):
            p.run("te", eng, build)

        @block.vector
        def _(eng):
            p.run("ve", eng, build)

        @block.scalar
        def _(eng):
            p.run("sc", eng, build)

        @block.gpsimd
        def _(eng):
            p.run("po", eng, build)

    return nc



def make_base(inputs):
    consts = _host_constants()
    base = {
        "L0_attn_Wq": inputs["L0_attn_Wq"],
        "L0_attn_Wk": inputs["L0_attn_Wk"],
        "L0_attn_Wv": inputs["L0_attn_Wv"],
        "L0_attn_Wo": inputs["L0_attn_Wo"],
        "L0_attn_Wg": inputs["L0_attn_Wg"],
        "L0_attn_Win": inputs["L0_attn_Win"],
        "dtb": np.ascontiguousarray(inputs["L0_attn_dt_bias"], dtype=np.float32).reshape(H, 1),
        "alog": np.ascontiguousarray(inputs["L0_attn_A_log_bias"], dtype=np.float32).reshape(H, 1),
        "gnw": np.ascontiguousarray(inputs["L0_attn_gnorm_w"], dtype=np.float32).reshape(128, 1),
        "L0_ffn_qweight_gu": inputs["L0_ffn_qweight_gu"],
        "L0_ffn_qzeros_gu": inputs["L0_ffn_qzeros_gu"],
        "L0_ffn_scales_gu": inputs["L0_ffn_scales_gu"],
        "L0_ffn_qweight_down": inputs["L0_ffn_qweight_down"],
        "L0_ffn_qzeros_down": inputs["L0_ffn_qzeros_down"],
        "L0_ffn_scales_down": inputs["L0_ffn_scales_down"],
    }
    base.update(consts)
    return base


def kernel(**inputs):
    x = np.ascontiguousarray(inputs["L0_attn_hidden_states"])
    base = make_base(inputs)
    nc = build_program()
    in_maps = [dict(base, x=np.ascontiguousarray(x[c])) for c in range(8)]
    res = run_bass_kernel_spmd(nc, in_maps, core_ids=list(range(8)))
    return np.stack([r["out"] for r in res.results], axis=0)
